# version 10
# baseline (speedup 1.0000x reference)
import numpy as np
import concourse.bass as bass
import concourse.mybir as mybir
from concourse.bass_utils import run_bass_kernel_spmd

F32 = mybir.dt.float32
AF = mybir.ActivationFunctionType
ALU = mybir.AluOpType
AX = mybir.AxisListType

NT = 8
TT = 512
POS = 256
DM = 1024
NFC = 8
SCALE_QK = float(128 ** -0.5)

C_IDN = 0
C_ONECOL = 128
C_ONEROW = 129
C_ANW = 257
C_MNW = 265
C_GNW = 273
C_BGK2 = 275
C_EPS_LN = 279
C_EPS_RMS = 280
C_MASKP = 281
C_IOTA256 = 409
C_IOTAJ = 665
C_NCOLS = 667


def host_consts(attn_norm_w, mlp_norm_w, gnorm_w, bgk2):
    c = np.zeros((128, C_NCOLS), np.float32)
    c[:, C_IDN:C_IDN + 128] = np.eye(128, dtype=np.float32)
    c[:, C_ONECOL] = 1.0
    c[0, C_ONEROW:C_ONEROW + 128] = 1.0
    c[:, C_ANW:C_ANW + 8] = np.asarray(attn_norm_w, np.float32).reshape(8, 128).T
    c[:, C_MNW:C_MNW + 8] = np.asarray(mlp_norm_w, np.float32).reshape(8, 128).T
    c[:, C_GNW:C_GNW + 2] = np.asarray(gnorm_w, np.float32).reshape(2, 128).T
    c[:, C_BGK2:C_BGK2 + 4] = np.asarray(bgk2, np.float32).reshape(4, 128).T
    c[:, C_EPS_LN] = 1e-5
    c[:, C_EPS_RMS] = 1e-6
    s, cc = np.mgrid[0:128, 0:128]
    mp = np.zeros((128, 128), np.float32)
    mp[(s < 64) & (cc < 64)] = (cc >= s)[(s < 64) & (cc < 64)]
    mp[(s < 64) & (cc >= 64)] = 1.0
    mp[(s >= 64) & (cc >= 64)] = (cc >= s)[(s >= 64) & (cc >= 64)]
    c[:, C_MASKP:C_MASKP + 128] = mp
    c[:, C_IOTA256:C_IOTA256 + 256] = np.arange(1, 257, dtype=np.float32)[None, :]
    c[:, C_IOTAJ] = np.arange(1, 129, dtype=np.float32)
    c[:, C_IOTAJ + 1] = np.arange(129, 257, dtype=np.float32)
    return c


def pack_lhsT(W):
    W = np.ascontiguousarray(np.asarray(W, np.float32))
    K, M = W.shape
    P = min(128, K)
    assert K % P == 0
    nk = K // P
    CH = min(128, M)
    assert M % CH == 0
    nm = M // CH
    out = np.empty((nm, P, nk * CH), np.float32)
    Wr = W.reshape(nk, P, M)
    for m in range(nm):
        blk = Wr[:, :, m * CH:(m + 1) * CH]
        out[m] = blk.transpose(1, 0, 2).reshape(P, nk * CH)
    return out


def pack_all_weights(ins):
    w = {}
    w["wq_p"] = pack_lhsT(ins["L0_attn_Wq"])
    w["wk_p"] = pack_lhsT(ins["L0_attn_Wk"])
    w["wgate_p"] = pack_lhsT(ins["L0_attn_W_gate"])
    w["wup_p"] = pack_lhsT(ins["L0_attn_W_up"])
    w["wdown_p"] = pack_lhsT(ins["L0_attn_W_down"])
    w["wo_p"] = pack_lhsT(ins["L0_attn_Wo"])
    w["wog_p"] = pack_lhsT(ins["L0_attn_Wg"])
    w["wgk1_p"] = pack_lhsT(ins["L0_attn_Wgk1"])[0]
    w["wgk2_p"] = pack_lhsT(ins["L0_attn_Wgk2"])
    w["wrouter_p"] = pack_lhsT(np.asarray(ins["L0_ffn_wg"], np.float32).T)[0]
    w["wv_nat"] = np.ascontiguousarray(np.asarray(ins["L0_attn_Wv"], np.float32))
    w1 = np.asarray(ins["L0_ffn_w1"], np.float32)
    w3 = np.asarray(ins["L0_ffn_w3"], np.float32)
    w2 = np.asarray(ins["L0_ffn_w2"], np.float32)
    w["moe1_p"] = np.stack([pack_lhsT(w1[e].T) for e in range(8)])
    w["moe3_p"] = np.stack([pack_lhsT(w3[e].T) for e in range(8)])
    w["moe2_p"] = np.stack([pack_lhsT(w2[e].T) for e in range(8)])
    w["moe2m_nat"] = np.ascontiguousarray(w2.transpose(0, 2, 1))
    return w


WEIGHT_SHAPES = {
    "wq_p": [4, 128, 1024], "wk_p": [4, 128, 1024],
    "wgate_p": [22, 128, 1024], "wup_p": [22, 128, 1024],
    "wdown_p": [8, 128, 2816], "wo_p": [8, 128, 1024], "wog_p": [8, 128, 1024],
    "wgk1_p": [128, 128], "wgk2_p": [4, 16, 128], "wrouter_p": [128, 64],
    "wv_nat": [1024, 1024],
    "moe1_p": [8, 16, 128, 1024], "moe3_p": [8, 16, 128, 1024],
    "moe2_p": [8, 8, 128, 2048], "moe2m_nat": [8, 2048, 1024],
}

ENGINES = ("tensor", "vector", "scalar", "sync", "gpsimd")
SPARSE_MOE = True
BF16_EXPERTS = True
BF = mybir.dt.bfloat16


class Ring:

    def __init__(self, prog, name, nslots, cols, dt):
        self.p = prog
        self.name = name
        self.R = nslots
        self.slots = [prog.nc.alloc_sbuf_tensor(f"{name}{j}", [128, cols], dt).ap()
                      for j in range(nslots)]
        self.sems = [prog.nc.alloc_semaphore(f"{name}s{j}") for j in range(nslots)]
        self.manifest = []
        self.n = 0
        self.retire = {}
        self._next_retire = 0

    def load(self, src_ap, part, width, gate=None):
        idx = self.n
        self.n += 1
        if getattr(self, "queue", "sync") == "gpsimd":
            ring = self
            def fill(E, src=src_ap, part=part, width=width, idx=idx, gate=gate):
                if idx >= ring.R:
                    sname, cnt = ring.retire[idx - ring.R]
                    if cnt > 0:
                        E.wait_ge(ring.p.sem[sname], cnt)
                if gate is not None:
                    for gn, gv in gate:
                        E.wait_ge(ring.p.sem[gn], gv)
                E.dma_start(out=ring.slots[idx % ring.R][0:part, 0:width],
                            in_=src).then_inc(ring.sems[idx % ring.R], 16)
            self.p.op("gpsimd", fill)
        else:
            self.p.fill_seq.append((self, src_ap, part, width, idx, gate))
        return self.slots[idx % self.R][0:part, 0:width], idx

    def wait_engine(self, eng, idx):
        sem = self.sems[idx % self.R]
        self.p.op(eng, lambda E, s=sem, v=16 * (idx // self.R + 1): E.wait_ge(s, v))

    def mark_retire(self, sem_name):
        while self._next_retire in self.retire:
            self._next_retire += 1
        self.retire[self._next_retire] = (sem_name, self.p.c[sem_name])
        self._next_retire += 1

    def mark_retire_idx(self, idx, sem_name):
        assert idx not in self.retire
        self.retire[idx] = (sem_name, self.p.c[sem_name])
CAP = 256
CJ0, CJ1 = 128, CAP - 128
RING_R = 6


class Prog:
    def __init__(self, ntiles=NT, debug_taps=()):
        self.nc = bass.Bass("TRN2", target_bir_lowering=False)
        nc = self.nc
        self.ntiles = ntiles
        self.debug_taps = set(debug_taps)
        self.ops = {e: [] for e in ENGINES}
        self.x = nc.dram_tensor("x", [2, DM, 2048], F32, kind="ExternalInput").ap()
        self.consts_d = nc.dram_tensor("consts", [128, C_NCOLS], F32,
                                       kind="ExternalInput").ap()
        self.wd = {k: nc.dram_tensor(k, v, F32, kind="ExternalInput").ap()
                   for k, v in WEIGHT_SHAPES.items()}
        self.out = nc.dram_tensor("out", [2, DM, 2048], F32, kind="ExternalOutput").ap()
        self.dbg = {}
        def sb(name, shape, dt=F32):
            return nc.alloc_sbuf_tensor(name, shape, dt).ap()
        self.X0 = sb("X0", [128, 4096])
        self.X1 = sb("X1", [128, 4096])
        self.BB = sb("BB", [128, 4096])
        self.YQ = sb("YQ", [128, 4096])
        self.VV = sb("VV", [128, 4096])
        self.HX = sb("HX", [128, 4096])
        self.PR = sb("PR", [128, 2048])
        self.S = sb("S", [128, 2048])
        self.SQ = sb("SQ", [128, 256])
        self.AMP = sb("AMP", [128, 128])
        self.GS = sb("GS", [128, 1056])
        self.EW = sb("EWs", [128, 52])
        self.WBC = sb("WBC", [128, 512])
        self.SILU = sb("SILU", [128, 512])
        self.SILU2 = sb("SILU2", [128, 512])
        self.EWF = sb("EWF", [8, 512])
        self.EWS8a = sb("EWS8a", [8, 512])
        self.EWS8b = sb("EWS8b", [8, 512])
        self.RNKT = sb("RNKT", [128, 32])
        self.U16 = sb("U16", [16, 512])
        self.ROWS = sb("ROWS", [1, 1024])
        self.RBC = sb("RBC", [128, 128])
        self.RNKBC = sb("RNKBC", [128, 512])
        self.ONC = sb("ONC", [128, C_NCOLS])
        self.WSMALL = sb("WSMALL", [128, 704])
        self.pb = [nc.alloc_psum_tensor(f"pb{j}", [128, 512], F32).ap()
                   for j in range(8)]
        self.bank_free = [("V", 0)] * 8
        names = ["sT", "sV", "sS", "sGC", "sGX", "sGS", "sGB0", "sGB1"]
        self.sem = {n: nc.alloc_semaphore(n) for n in names}
        self.c = {n: 0 for n in self.sem}
        self.fill_seq = []
        self.wring = Ring(self, "wr", 4, 1024, F32)
        if BF16_EXPERTS:
            self.bring = Ring(self, "wb", 18, 1024, BF)
            self.bring.queue = "gpsimd"
            self.GHB = nc.alloc_sbuf_tensor("GHB", [128, 2048], BF).ap()
            self.GLB = nc.alloc_sbuf_tensor("GLB", [128, 2048], BF).ap()
            self.PHB = nc.alloc_sbuf_tensor("PHB", [128, 512], BF).ap()
            self.PLB = nc.alloc_sbuf_tensor("PLB", [128, 512], BF).ap()
            self.w13h = nc.dram_tensor("w13h", [8, 2, 16, 128, 1024], BF).ap()
            self.w13l = nc.dram_tensor("w13l", [8, 2, 16, 128, 1024], BF).ap()
            self.w2h = nc.dram_tensor("w2h", [8, 16, 128, 1024], BF).ap()
            self.w2l = nc.dram_tensor("w2l", [8, 16, 128, 1024], BF).ap()

    def op(self, eng, fn):
        self.ops[eng].append(fn)

    def wait(self, eng, sem, val):
        if val <= 0:
            return
        s = self.sem[sem]
        self.op(eng, lambda E, s=s, v=val: E.wait_ge(s, v))

    def _inc(self, eng, fn, sem, n=1):
        s = self.sem[sem]
        self.op(eng, lambda E, fn=fn, s=s, n=n: fn(E).then_inc(s, n))
        self.c[sem] += n
        return self.c[sem]

    def t(self, fn):
        return self._inc("tensor", fn, "sT")

    def v(self, fn):
        return self._inc("vector", fn, "sV")

    def s(self, fn):
        return self._inc("scalar", fn, "sS")

    def plain(self, eng, fn):
        self.op(eng, fn)

    def vdrain(self):
        self.op("vector", lambda E: E.drain())

    def wait_counts(self, eng, counts):
        for k, v in counts.items():
            self.wait(eng, k, v)

    def barrier_others(self, eng):
        m = {"tensor": "sT", "vector": "sV", "scalar": "sS"}
        for e2, s2 in m.items():
            if e2 != eng:
                self.wait(eng, s2, self.c[s2])

    def bank_acquire(self, j):
        sem, cnt = self.bank_free[j]
        self.wait("tensor", sem, cnt)

    def bank_done_evac(self, j, sem):
        self.bank_free[j] = (sem, self.c[sem])

    def wload(self, src_ap, part, width):
        return self.wring.load(src_ap, part, width)

    def wwait(self, idx):
        self.wring.wait_engine("tensor", idx)

    def wretire(self):
        self.wring.mark_retire("sT")

    def gdma(self, out_ap, in_ap, waits=None, sem="sGX"):
        s = self.sem[sem]
        def fn(E, out_ap=out_ap, in_ap=in_ap, waits=waits or {}):
            for k, v in waits.items():
                E.wait_ge(self.sem[k], v)
            E.dma_start(out=out_ap, in_=in_ap).then_inc(s, 16)
        self.op("gpsimd", fn)
        self.c[sem] += 16
        return self.c[sem]

    def chunk(self, buf, c, width=512):
        return buf[:, c * width:(c + 1) * width]

    def ccol(self, col, p0=0, p1=128):
        return self.ONC[p0:p1, col:col + 1]

    def emit_preamble(self):
        self.gdma(self.ONC[:, 0:C_NCOLS], self.consts_d, sem="sGC")
        self.gdma(self.WSMALL[:, 0:64], self.wd["wrouter_p"], sem="sGC")
        self.gdma(self.WSMALL[:, 64:192], self.wd["wgk1_p"], sem="sGC")
        for m in range(4):
            self.gdma(self.WSMALL[0:16, 192 + m * 128:192 + (m + 1) * 128],
                      self.wd["wgk2_p"][m], sem="sGC")
        g = self.c["sGC"]
        self.g_consts = g
        self.v(lambda E: E.memset(self.S, 0.0))
        self.v(lambda E: E.memset(self.AMP, 0.0))
        for e in ("tensor", "vector", "scalar"):
            self.wait(e, "sGC", self.g_consts)

    def load_xtile(self, i, waits=None):
        last = 0
        for fc in range(NFC):
            for sq in range(2):
                dst = self.X0[:, fc * 512 + sq * POS: fc * 512 + (sq + 1) * POS]
                src = self.x[sq, fc * 128:(fc + 1) * 128, i * POS:(i + 1) * POS]
                last = self.gdma(dst, src, sem="sGX",
                                 waits=waits if (fc == 0 and sq == 0) else None)
        return last

    def store_tile(self, i, srcbuf):
        last = 0
        w = {"sV": self.c["sV"]}
        for fc in range(NFC):
            for sq in range(2):
                src = srcbuf[:, fc * 512 + sq * POS: fc * 512 + (sq + 1) * POS]
                dst = self.out[sq, fc * 128:(fc + 1) * 128, i * POS:(i + 1) * POS]
                last = self.gdma(dst, src, sem="sGS",
                                 waits=w if (fc == 0 and sq == 0) else None)
        return last

    def emit_norm(self, src, dst, kind, w_col0=None, eps_col=C_EPS_RMS,
                  src_ready=None):
        rows = self.ROWS
        m_row = rows[0:1, 0:512]; r_row = rows[0:1, 512:1024]
        t1 = self.SILU[0:1, 0:512]; t2 = self.SILU2[0:1, 0:512]
        if src_ready:
            self.wait_counts("tensor", src_ready)
            self.wait_counts("scalar", src_ready)
        bsum, bsq, bb1, bb2 = 0, 1, 2, 3
        sq_cnt = []
        for c in range(NFC):
            sq_cnt.append(self.s(lambda E, c=c: E.activation(
                out=self.chunk(dst, c), in_=self.chunk(src, c), func=AF.Square)))
        if kind == "ln":
            self.bank_acquire(bsum)
            for c in range(NFC):
                cnt = self.t(lambda E, c=c: E.matmul(
                    self.pb[bsum][0:1, :], self.ONC[:, C_ONECOL:C_ONECOL + 1],
                    self.chunk(src, c), start=(c == 0), stop=(c == NFC - 1)))
            mm_sum_end = cnt
        self.bank_acquire(bsq)
        for c in range(NFC):
            self.wait("tensor", "sS", sq_cnt[c])
            cnt = self.t(lambda E, c=c: E.matmul(
                self.pb[bsq][0:1, :], self.ONC[:, C_ONECOL:C_ONECOL + 1],
                self.chunk(dst, c), start=(c == 0), stop=(c == NFC - 1)))
        mm_sq_end = cnt
        if kind == "ln":
            self.wait("scalar", "sT", mm_sum_end)
            self.s(lambda E: E.activation(out=m_row, in_=self.pb[bsum][0:1, :],
                                          func=AF.Copy, scale=1.0 / DM))
        self.wait("scalar", "sT", mm_sq_end)
        s_msq = self.s(lambda E: E.activation(out=t1, in_=self.pb[bsq][0:1, :],
                                              func=AF.Copy, scale=1.0 / DM))
        self.bank_done_evac(bsq, "sS")
        self.wait("vector", "sS", s_msq)
        if kind == "ln":
            self.bank_done_evac(bsum, "sS")
            self.v(lambda E: E.tensor_tensor(out=t2, in0=m_row, in1=m_row,
                                             op=ALU.mult))
            vv = self.v(lambda E: E.tensor_tensor(out=t1, in0=t1, in1=t2,
                                                  op=ALU.subtract))
            self.wait("scalar", "sV", vv)
        s_sd = self.s(lambda E: E.activation(
            out=t2, in_=t1, func=AF.Sqrt,
            bias=self.ONC[0:1, eps_col:eps_col + 1]))
        self.wait("vector", "sS", s_sd)
        self.vdrain()
        v_r = self.v(lambda E: E.reciprocal(out=r_row, in_=t2))
        self.vdrain()
        self.wait("tensor", "sV", v_r)
        if kind == "ln":
            self.bank_acquire(bb1)
            self.t(lambda E: E.matmul(self.pb[bb1], self.ONC[0:1, C_ONEROW:C_ONEROW + 128],
                                      m_row, start=True, stop=True))
        self.bank_acquire(bb2)
        mm_bc = self.t(lambda E: E.matmul(self.pb[bb2],
                                          self.ONC[0:1, C_ONEROW:C_ONEROW + 128],
                                          r_row, start=True, stop=True))
        self.wait("vector", "sT", mm_bc)
        for c in range(NFC):
            if kind == "ln":
                self.v(lambda E, c=c: E.tensor_tensor(
                    out=self.chunk(dst, c), in0=self.chunk(src, c),
                    in1=self.pb[bb1], op=ALU.subtract))
                vv = self.v(lambda E, c=c: E.tensor_tensor(
                    out=self.chunk(dst, c), in0=self.chunk(dst, c),
                    in1=self.pb[bb2], op=ALU.mult))
            else:
                vv = self.v(lambda E, c=c: E.scalar_tensor_tensor(
                    out=self.chunk(dst, c), in0=self.chunk(src, c),
                    scalar=self.ccol(w_col0 + c), in1=self.pb[bb2],
                    op0=ALU.mult, op1=ALU.mult))
        if kind == "ln":
            self.bank_done_evac(bb1, "sV")
        self.bank_done_evac(bb2, "sV")
        return vv

    def emit_routing(self, src):
        self.wait("tensor", "sV", self.c["sV"])
        slot = self.WSMALL[:, 0:64]
        EW = self.EW
        LGs = EW[:, 0:8]; m8 = EW[:, 8:16]; mr = EW[:, 16:24]
        repl = EW[:, 24:32]; sel = EW[:, 32:40]; eraw = EW[:, 40:48]
        nl1 = EW[:, 48:49]; den = EW[:, 49:50]; rec = EW[:, 50:51]
        e2c = EW[:, 51:52]
        for g in range(4):
            blg, btr = 5, 4
            self.bank_acquire(blg)
            for kc in range(NFC):
                mm_lg = self.t(lambda E, g=g, kc=kc, slot=slot: E.matmul(
                    self.pb[blg][:, 0:8],
                    src[:, kc * 512 + g * 128: kc * 512 + (g + 1) * 128],
                    slot[:, kc * 8:(kc + 1) * 8],
                    start=(kc == 0), stop=(kc == NFC - 1)))
            self.wait("vector", "sT", mm_lg)
            self.v(lambda E: E.tensor_copy(out=LGs, in_=self.pb[blg][:, 0:8]))
            self.bank_done_evac(blg, "sV")
            self.vdrain()
            self.v(lambda E: E.max(out=m8, in_=LGs))
            self.vdrain()
            self.v(lambda E: E.memset(mr, -1e30))
            self.v(lambda E: E.tensor_copy(out=mr[:, 0:2], in_=m8[:, 0:2]))
            self.vdrain()
            self.v(lambda E: E.match_replace(out=repl, in_to_replace=mr,
                                             in_values=LGs, imm_value=-1e30))
            self.vdrain()
            self.v(lambda E: E.tensor_tensor(out=sel, in0=LGs, in1=repl,
                                             op=ALU.not_equal))
            vnl = self.v(lambda E: E.tensor_scalar(out=nl1, in0=m8[:, 0:1],
                                                   scalar1=-1.0, scalar2=None,
                                                   op0=ALU.mult))
            self.wait("scalar", "sV", vnl)
            self.s(lambda E: E.activation(out=eraw, in_=LGs, func=AF.Exp,
                                          bias=nl1))
            se2 = self.s(lambda E: E.activation(out=e2c, in_=m8[:, 1:2],
                                                func=AF.Exp, bias=nl1))
            self.wait("vector", "sS", se2)
            self.v(lambda E: E.tensor_scalar(out=den, in0=e2c, scalar1=1.0,
                                             scalar2=None, op0=ALU.add))
            self.vdrain()
            self.v(lambda E: E.reciprocal(out=rec, in_=den))
            self.vdrain()
            self.v(lambda E: E.tensor_tensor(out=eraw, in0=eraw, in1=sel,
                                             op=ALU.mult))
            vew = self.v(lambda E: E.tensor_scalar(out=eraw, in0=eraw,
                                                   scalar1=rec, scalar2=None,
                                                   op0=ALU.mult))
            self.bank_acquire(btr)
            self.wait("tensor", "sV", vew)
            mmt = self.t(lambda E: E.transpose(
                self.pb[btr][0:8, 0:128], eraw,
                self.ONC[:, C_IDN:C_IDN + 128]))
            self.wait("vector", "sT", mmt)
            self.v(lambda E, g=g: E.tensor_copy(
                out=self.EWF[:, g * 128:(g + 1) * 128],
                in_=self.pb[btr][0:8, 0:128]))
            self.bank_done_evac(btr, "sV")

    def emit_moe(self, src, dst):
        self.wait("tensor", "sV", self.c["sV"])
        ewrow = self.U16[0:1, 0:512]
        first = True
        for e in range(8):
            brow, bbc = 4, 5
            self.bank_acquire(brow)
            m1 = self.t(lambda E, e=e: E.matmul(
                self.pb[brow][0:1, 0:512],
                self.ONC[0:8, C_IDN + e:C_IDN + e + 1],
                self.EWF, start=True, stop=True))
            self.wait("scalar", "sT", m1)
            sc = self.s(lambda E: E.activation(out=ewrow, in_=self.pb[brow][0:1, 0:512],
                                               func=AF.Copy))
            self.bank_done_evac(brow, "sS")
            self.bank_acquire(bbc)
            self.wait("tensor", "sS", sc)
            m2 = self.t(lambda E: E.matmul(
                self.pb[bbc], self.ONC[0:1, C_ONEROW:C_ONEROW + 128],
                ewrow, start=True, stop=True))
            self.wait("vector", "sT", m2)
            vbc = self.v(lambda E: E.tensor_copy(out=self.WBC, in_=self.pb[bbc]))
            self.bank_done_evac(bbc, "sV")
            for hf in range(2):
                for m in range(8):
                    mi = hf * 8 + m
                    s1, i1 = self.wload(self.wd["moe1_p"][e, mi], 128, 1024)
                    s3, i3 = self.wload(self.wd["moe3_p"][e, mi], 128, 1024)
                    bA, bB = (0, 1) if (m % 2 == 0) else (2, 3)
                    self.bank_acquire(bA)
                    self.wwait(i1)
                    for kc in range(NFC):
                        mmA = self.t(lambda E, kc=kc, s1=s1, bA=bA: E.matmul(
                            self.pb[bA], s1[:, kc * 128:(kc + 1) * 128],
                            self.chunk(src, kc),
                            start=(kc == 0), stop=(kc == NFC - 1)))
                    self.wretire()
                    self.bank_acquire(bB)
                    self.wwait(i3)
                    for kc in range(NFC):
                        mmB = self.t(lambda E, kc=kc, s3=s3, bB=bB: E.matmul(
                            self.pb[bB], s3[:, kc * 128:(kc + 1) * 128],
                            self.chunk(src, kc),
                            start=(kc == 0), stop=(kc == NFC - 1)))
                    self.wretire()
                    self.wait("scalar", "sT", mmA)
                    ss = self.s(lambda E, bA=bA: E.activation(
                        out=self.SILU, in_=self.pb[bA], func=AF.Silu))
                    self.bank_done_evac(bA, "sS")
                    self.wait("vector", "sS", ss)
                    self.wait("vector", "sV", vbc)
                    self.v(lambda E: E.tensor_tensor(out=self.SILU2, in0=self.SILU,
                                                     in1=self.WBC, op=ALU.mult))
                    self.wait("vector", "sT", mmB)
                    vp = self.v(lambda E, m=m, bB=bB: E.tensor_tensor(
                        out=self.PR[:, m * 512:(m + 1) * 512],
                        in0=self.pb[bB], in1=self.SILU2, op=ALU.mult))
                    self.bank_done_evac(bB, "sV")
                for f in range(NFC):
                    s2, i2 = self.wload(
                        self.wd["moe2_p"][e, f][:, hf * 1024:(hf + 1) * 1024],
                        128, 1024)
                    bY = 6 if (f % 2 == 0) else 7
                    self.bank_acquire(bY)
                    self.wwait(i2)
                    self.wait("tensor", "sV", vp)
                    for kc in range(NFC):
                        mmY = self.t(lambda E, kc=kc, s2=s2, bY=bY: E.matmul(
                            self.pb[bY], s2[:, kc * 128:(kc + 1) * 128],
                            self.PR[:, kc * 512:(kc + 1) * 512],
                            start=(kc == 0), stop=(kc == NFC - 1)))
                    self.wretire()
                    self.wait("vector", "sT", mmY)
                    if first:
                        self.v(lambda E, f=f, bY=bY: E.tensor_copy(
                            out=self.chunk(dst, f), in_=self.pb[bY]))
                    else:
                        self.v(lambda E, f=f, bY=bY: E.tensor_tensor(
                            out=self.chunk(dst, f), in0=self.pb[bY],
                            in1=self.chunk(dst, f), op=ALU.add))
                    self.bank_done_evac(bY, "sV")
                first = False

    def emit_bf16_staging(self):
        jobs = []
        for e in range(8):
            for mat in range(2):
                nm = "moe1_p" if mat == 0 else "moe3_p"
                for m in range(16):
                    jobs.append((self.wd[nm][e, m],
                                 self.w13h[e, mat, m], self.w13l[e, mat, m]))
        for e in range(8):
            for kc in range(16):
                jobs.append((self.wd["moe2m_nat"][e, kc * 128:(kc + 1) * 128, :],
                             self.w2h[e, kc], self.w2l[e, kc]))
        for j, (src, dh, dl) in enumerate(jobs):
            slot, idx = self.wring.load(src, 128, 1024)
            self.wring.wait_engine("vector", idx)
            pp = j % 2
            sem = "sGB0" if pp == 0 else "sGB1"
            hi = self.GHB[:, pp * 1024:(pp + 1) * 1024]
            lo = self.GLB[:, pp * 1024:(pp + 1) * 1024]
            if j >= 2:
                self.wait("vector", sem, 32 * (j // 2))
            self.v(lambda E, hi=hi, slot=slot: E.tensor_copy(out=hi, in_=slot))
            vv = self.v(lambda E, hi=hi, lo=lo, slot=slot: E.tensor_tensor(
                out=lo, in0=slot, in1=hi, op=ALU.subtract))
            self.wring.mark_retire("sV")
            self.gdma(dh, hi, waits={"sV": vv}, sem=sem)
            self.gdma(dl, lo, sem=sem)
        self.stage_gate = [("sGB0", self.c["sGB0"]), ("sGB1", self.c["sGB1"])]
        self.first_bload = True

    def bload(self, src, part=128, width=1024):
        gate = self.stage_gate if self.first_bload else None
        self.first_bload = False
        return self.bring.load(src, part, width, gate=gate)

    def emit_moe_pre(self, src, ytm):
        self.wait("tensor", "sV", self.c["sV"])
        for g in range(4):
            for fc in range(NFC):
                b = 6 + (fc % 2)
                self.bank_acquire(b)
                mmt = self.t(lambda E, g=g, fc=fc, b=b: E.transpose(
                    self.pb[b][:, 0:128],
                    src[:, fc * 512 + g * 128:fc * 512 + (g + 1) * 128],
                    self.ONC[:, C_IDN:C_IDN + 128]))
                self.wait("vector", "sT", mmt)
                self.v(lambda E, g=g, fc=fc, b=b: E.tensor_copy(
                    out=ytm[:, g * 1024 + fc * 128:g * 1024 + (fc + 1) * 128],
                    in_=self.pb[b][:, 0:128]))
                self.bank_done_evac(b, "sV")

    def emit_moe_sparse(self, src, dst, ytm, gye):
        self.wait("tensor", "sV", self.c["sV"])
        ewrow = self.U16[0:1, 0:512]
        SEL = self.EW[:, 0:8]
        SELF_ = self.nc
        selF = self.EWS8a
        rankF = self.EWS8b
        self.v(lambda E: E.tensor_scalar(out=selF, in0=self.EWF, scalar1=0.0,
                                         scalar2=None, op0=ALU.not_equal))
        self.v(lambda E: E.tensor_tensor_scan(out=rankF, data0=selF, data1=selF,
                                              initial=0.0, op0=ALU.add,
                                              op1=ALU.bypass))
        vrk = self.v(lambda E: E.tensor_tensor(out=rankF, in0=rankF, in1=selF,
                                               op=ALU.mult))
        for g in range(4):
            self.bank_acquire(6)
            self.wait("tensor", "sV", vrk)
            mmt = self.t(lambda E, g=g: E.transpose(
                self.pb[6][0:128, 0:8], rankF[:, g * 128:(g + 1) * 128],
                self.ONC[0:8, C_IDN:C_IDN + 8]))
            self.wait("vector", "sT", mmt)
            self.v(lambda E, g=g: E.tensor_copy(
                out=self.RNKT[:, g * 8:(g + 1) * 8], in_=self.pb[6][0:128, 0:8]))
            self.bank_done_evac(6, "sV")
        GFM = gye
        first = True
        for e in range(8):
            for srcrow, dstbc in ((self.EWF, self.WBC), (rankF, self.RNKBC)):
                brow, bbc = 4, 5
                self.bank_acquire(brow)
                m1 = self.t(lambda E, e=e, srcrow=srcrow: E.matmul(
                    self.pb[brow][0:1, 0:512],
                    self.ONC[0:8, C_IDN + e:C_IDN + e + 1],
                    srcrow, start=True, stop=True))
                self.wait("scalar", "sT", m1)
                sc = self.s(lambda E: E.activation(
                    out=ewrow, in_=self.pb[brow][0:1, 0:512], func=AF.Copy))
                self.bank_done_evac(brow, "sS")
                self.bank_acquire(bbc)
                self.wait("tensor", "sS", sc)
                m2 = self.t(lambda E: E.matmul(
                    self.pb[bbc], self.ONC[0:1, C_ONEROW:C_ONEROW + 128],
                    ewrow, start=True, stop=True))
                self.wait("vector", "sT", m2)
                self.v(lambda E, dstbc=dstbc: E.tensor_copy(out=dstbc,
                                                            in_=self.pb[bbc]))
                self.bank_done_evac(bbc, "sV")
            for g in range(4):
                vpb = self.v(lambda E, g=g, e=e: E.tensor_scalar(
                    out=self.PR[:, g * 256:g * 256 + CAP],
                    in0=self.ONC[:, C_IOTA256:C_IOTA256 + CAP],
                    scalar1=self.RNKT[:, g * 8 + e:g * 8 + e + 1],
                    scalar2=None, op0=ALU.is_equal))
            for fc in range(NFC):
                b = fc % 2
                self.bank_acquire(b)
                self.wait("tensor", "sV", vpb)
                for g in range(4):
                    mmg = self.t(lambda E, g=g, fc=fc, b=b: E.matmul(
                        self.pb[b][:, 0:CAP],
                        ytm[:, g * 1024 + fc * 128:g * 1024 + (fc + 1) * 128],
                        self.PR[:, g * 256:g * 256 + CAP],
                        start=(g == 0), stop=(g == 3)))
                self.wait("vector", "sT", mmg)
                if BF16_EXPERTS:
                    self.v(lambda E, fc=fc, b=b: E.tensor_copy(
                        out=self.GHB[:, fc * 256:fc * 256 + CAP],
                        in_=self.pb[b][:, 0:CAP]))
                    self.v(lambda E, fc=fc, b=b: E.tensor_tensor(
                        out=self.GLB[:, fc * 256:fc * 256 + CAP],
                        in0=self.pb[b][:, 0:CAP],
                        in1=self.GHB[:, fc * 256:fc * 256 + CAP],
                        op=ALU.subtract))
                else:
                    self.v(lambda E, fc=fc, b=b: E.tensor_copy(
                        out=GFM[:, fc * 256:fc * 256 + CAP],
                        in_=self.pb[b][:, 0:CAP]))
                self.bank_done_evac(b, "sV")
            if BF16_EXPERTS:
                for b4 in (4, 5, 6, 7):
                    self.bank_acquire(b4)
                prev = None
                for m in range(17):
                    if m < 16:
                        w1h, i1h = self.bload(self.w13h[e, 0, m])
                        w1l, i1l = self.bload(self.w13l[e, 0, m])
                        w3h, i3h = self.bload(self.w13h[e, 1, m])
                        w3l, i3l = self.bload(self.w13l[e, 1, m])
                        w2hs, i2h = self.bload(self.w2h[e, m])
                        w2ls, i2l = self.bload(self.w2l[e, m])
                        bA, bB = (0, 1) if (m % 2 == 0) else (2, 3)
                        self.bank_acquire(bA)
                        self.bring.wait_engine("tensor", i1h)
                        self.bring.wait_engine("tensor", i1l)
                        np_ = 0
                        for kc in range(NFC):
                            for wsl, gsl in ((w1h, self.GHB), (w1h, self.GLB),
                                             (w1l, self.GHB)):
                                mmA = self.t(lambda E, kc=kc, wsl=wsl, gsl=gsl,
                                             bA=bA, np_=np_: E.matmul(
                                    self.pb[bA][:, 0:CAP],
                                    wsl[:, kc * 128:(kc + 1) * 128],
                                    gsl[:, kc * 256:kc * 256 + CAP],
                                    start=(np_ == 0), stop=(np_ == 23)))
                                np_ += 1
                        self.bring.mark_retire_idx(i1h, "sT")
                        self.bring.mark_retire_idx(i1l, "sT")
                        self.bank_acquire(bB)
                        self.bring.wait_engine("tensor", i3h)
                        self.bring.wait_engine("tensor", i3l)
                        np_ = 0
                        for kc in range(NFC):
                            for wsl, gsl in ((w3h, self.GHB), (w3h, self.GLB),
                                             (w3l, self.GHB)):
                                mmB = self.t(lambda E, kc=kc, wsl=wsl, gsl=gsl,
                                             bB=bB, np_=np_: E.matmul(
                                    self.pb[bB][:, 0:CAP],
                                    wsl[:, kc * 128:(kc + 1) * 128],
                                    gsl[:, kc * 256:kc * 256 + CAP],
                                    start=(np_ == 0), stop=(np_ == 23)))
                                np_ += 1
                        self.bring.mark_retire_idx(i3h, "sT")
                        self.bring.mark_retire_idx(i3l, "sT")
                    if prev is not None:
                        pm, pw2h, pw2l, pi2h, pi2l, pvph, pp = prev
                        self.wait("tensor", "sV", pvph)
                        self.bring.wait_engine("tensor", pi2h)
                        self.bring.wait_engine("tensor", pi2l)
                        for jm in range(2):
                            jw = CJ0 if jm == 0 else CJ1
                            for fh in range(2):
                                b = 4 + jm * 2 + fh
                                np_ = 0
                                for psl, wsl in ((self.PHB, pw2h),
                                                 (self.PHB, pw2l),
                                                 (self.PLB, pw2h)):
                                    mmY = self.t(lambda E, jm=jm, fh=fh, b=b,
                                                 jw=jw, psl=psl, wsl=wsl,
                                                 pm=pm, pp=pp,
                                                 np_=np_: E.matmul(
                                        self.pb[b][0:jw, :],
                                        psl[:, pp * 256 + jm * 128:
                                            pp * 256 + jm * 128 + jw],
                                        wsl[:, fh * 512:(fh + 1) * 512],
                                        start=(pm == 0 and np_ == 0),
                                        stop=(pm == 15 and np_ == 2)))
                                    np_ += 1
                        self.bring.mark_retire_idx(pi2h, "sT")
                        self.bring.mark_retire_idx(pi2l, "sT")
                    if m < 16:
                        pp = m % 2
                        self.wait("scalar", "sT", mmA)
                        ss = self.s(lambda E, bA=bA: E.activation(
                            out=self.SILU[:, 0:CAP], in_=self.pb[bA][:, 0:CAP],
                            func=AF.Silu))
                        self.bank_done_evac(bA, "sS")
                        self.wait("vector", "sS", ss)
                        self.wait("vector", "sT", mmB)
                        self.v(lambda E, bB=bB: E.tensor_tensor(
                            out=self.SILU2[:, 0:CAP], in0=self.pb[bB][:, 0:CAP],
                            in1=self.SILU[:, 0:CAP], op=ALU.mult))
                        self.bank_done_evac(bB, "sV")
                        self.v(lambda E, pp=pp: E.tensor_copy(
                            out=self.PHB[:, pp * 256:pp * 256 + CAP],
                            in_=self.SILU2[:, 0:CAP]))
                        vph = self.v(lambda E, pp=pp: E.tensor_tensor(
                            out=self.PLB[:, pp * 256:pp * 256 + CAP],
                            in0=self.SILU2[:, 0:CAP],
                            in1=self.PHB[:, pp * 256:pp * 256 + CAP],
                            op=ALU.subtract))
                        prev = (m, w2hs, w2ls, i2h, i2l, vph, pp)
                self.wait("vector", "sT", self.c["sT"])
                for jm in range(2):
                    jw = CJ0 if jm == 0 else CJ1
                    for fh in range(2):
                        b = 4 + jm * 2 + fh
                        self.v(lambda E, jm=jm, fh=fh, b=b, jw=jw: E.tensor_copy(
                            out=gye[0:jw, 2048 + jm * 1024 + fh * 512:
                                    2048 + jm * 1024 + (fh + 1) * 512],
                            in_=self.pb[b][0:jw, :]))
                        self.bank_done_evac(b, "sV")
            else:
                raise NotImplementedError("fp32 sparse experts removed; "
                                          "use dense emit_moe instead")
            for jc in range(2):
                self.v(lambda E, jc=jc: E.tensor_scalar(
                    out=self.PR[:, 1024 + jc * 512:1024 + (jc + 1) * 512],
                    in0=self.RNKBC, scalar1=self.ccol(C_IOTAJ + jc),
                    scalar2=None, op0=ALU.is_equal))
                vptw = self.v(lambda E, jc=jc: E.tensor_tensor(
                    out=self.PR[:, 1024 + jc * 512:1024 + (jc + 1) * 512],
                    in0=self.PR[:, 1024 + jc * 512:1024 + (jc + 1) * 512],
                    in1=self.WBC, op=ALU.mult))
            for fc in range(NFC):
                b = 4 + (fc % 2)
                self.bank_acquire(b)
                self.wait("tensor", "sV", vptw)
                for jc in range(2):
                    jw = CJ0 if jc == 0 else CJ1
                    mms = self.t(lambda E, jc=jc, fc=fc, b=b, jw=jw: E.matmul(
                        self.pb[b],
                        gye[0:jw, 2048 + jc * 1024 + fc * 128:
                            2048 + jc * 1024 + (fc + 1) * 128],
                        self.PR[0:jw, 1024 + jc * 512:1024 + (jc + 1) * 512],
                        start=(jc == 0), stop=(jc == 1)))
                self.wait("vector", "sT", mms)
                if first:
                    self.v(lambda E, fc=fc, b=b: E.tensor_copy(
                        out=self.chunk(dst, fc), in_=self.pb[b]))
                else:
                    self.v(lambda E, fc=fc, b=b: E.tensor_tensor(
                        out=self.chunk(dst, fc), in0=self.pb[b],
                        in1=self.chunk(dst, fc), op=ALU.add))
                self.bank_done_evac(b, "sV")
            first = False

    def emit_resid_half(self, base, delta, dst):
        for c in range(NFC):
            vv = self.v(lambda E, c=c: E.scalar_tensor_tensor(
                out=self.chunk(dst, c), in0=self.chunk(delta, c), scalar=0.5,
                in1=self.chunk(base, c), op0=ALU.mult, op1=ALU.add))
        return vv

    def emit_projections(self, src):
        self.wait("tensor", "sV", self.c["sV"])
        for pk, base in (("wq_p", 0), ("wk_p", 2048)):
            for m in range(4):
                slot, idx = self.wload(self.wd[pk][m], 128, 1024)
                b = m % 2
                self.bank_acquire(b)
                self.wwait(idx)
                for kc in range(NFC):
                    mm = self.t(lambda E, kc=kc, slot=slot, b=b: E.matmul(
                        self.pb[b], slot[:, kc * 128:(kc + 1) * 128],
                        self.chunk(src, kc),
                        start=(kc == 0), stop=(kc == NFC - 1)))
                self.wretire()
                self.wait("vector", "sT", mm)
                self.v(lambda E, m=m, b=b, base=base: E.tensor_copy(
                    out=self.YQ[:, base + m * 512:base + (m + 1) * 512],
                    in_=self.pb[b]))
                self.bank_done_evac(b, "sV")
        for m in range(NFC):
            slot, idx = self.wload(self.wd["wog_p"][m], 128, 1024)
            b = m % 2
            self.bank_acquire(b)
            self.wwait(idx)
            for kc in range(NFC):
                mm = self.t(lambda E, kc=kc, slot=slot, b=b: E.matmul(
                    self.pb[b], slot[:, kc * 128:(kc + 1) * 128],
                    self.chunk(src, kc),
                    start=(kc == 0), stop=(kc == NFC - 1)))
            self.wretire()
            self.wait("scalar", "sT", mm)
            self.s(lambda E, m=m, b=b: E.activation(
                out=self.X0[:, m * 512:(m + 1) * 512], in_=self.pb[b],
                func=AF.Silu))
            self.bank_done_evac(b, "sS")
        self.bank_acquire(2)
        for kc in range(NFC):
            mm = self.t(lambda E, kc=kc: E.matmul(
                self.pb[2][0:16, :],
                self.WSMALL[:, 64 + kc * 16:64 + (kc + 1) * 16],
                self.chunk(src, kc), start=(kc == 0), stop=(kc == NFC - 1)))
        self.wait("vector", "sT", mm)
        self.v(lambda E: E.tensor_copy(out=self.U16, in_=self.pb[2][0:16, :]))
        self.bank_done_evac(2, "sV")
        uvr = self.c["sV"]
        for m in range(4):
            b = 3 + (m % 2)
            self.bank_acquire(b)
            self.wait("tensor", "sV", uvr)
            mm = self.t(lambda E, m=m, b=b: E.matmul(
                self.pb[b], self.WSMALL[0:16, 192 + m * 128:192 + (m + 1) * 128],
                self.U16, start=True, stop=True))
            self.wait("scalar", "sT", mm)
            self.s(lambda E, m=m, b=b: E.activation(
                out=self.PR[:, m * 512:(m + 1) * 512],
                in_=self.pb[b], func=AF.Sigmoid,
                bias=self.ccol(C_BGK2 + m)))
            sl = self.s(lambda E, m=m: E.activation(
                out=self.PR[:, m * 512:(m + 1) * 512],
                in_=self.PR[:, m * 512:(m + 1) * 512],
                func=AF.Ln))
            self.bank_done_evac(b, "sS")
        self.gl_ready = sl
        for kc in range(NFC):
            slot, idx = self.wload(
                self.wd["wv_nat"][kc * 128:(kc + 1) * 128, :], 128, 1024)
            self.wwait(idx)
            for pr in range(4):
                for half in range(2):
                    b = pr * 2 + half
                    if kc == 0:
                        self.bank_acquire(b)
                    mm = self.t(lambda E, kc=kc, pr=pr, half=half, b=b,
                                slot=slot: E.matmul(
                        self.pb[b],
                        src[:, kc * 512 + pr * 128:kc * 512 + (pr + 1) * 128],
                        slot[:, half * 512:(half + 1) * 512],
                        start=(kc == 0), stop=(kc == NFC - 1)))
            self.wretire()
        self.wait("vector", "sT", mm)
        for pr in range(4):
            for half in range(2):
                b = pr * 2 + half
                self.v(lambda E, pr=pr, half=half, b=b: E.tensor_copy(
                    out=self.VV[:, pr * 1024 + half * 512:
                                pr * 1024 + (half + 1) * 512],
                    in_=self.pb[b]))
                self.bank_done_evac(b, "sV")

    def emit_gla(self):
        GS = self.GS
        BP = GS[:, 0:128]; EB = GS[:, 128:256]; EM = GS[:, 256:384]
        KDX = GS[:, 384:512]; QEX = GS[:, 512:640]; KADJ = GS[:, 640:768]
        KTs = GS[:, 768:896]
        gA = GS[:, 896:897]; gB = GS[:, 897:898]
        gAB = GS[:, 898:899]; giA = GS[:, 899:900]
        ONF = GS[:, 900:1028]
        rows = self.ROWS
        sd_row = rows[0:1, 0:128]; r_row = rows[0:1, 512:640]
        if not hasattr(self, "s_last"):
            self.s_last = {}
        for pr in range(4):
            seq = pr // 2
            for hd in range(4):
                glsl = self.PR[:, hd * 512 + pr * 128:
                               hd * 512 + (pr + 1) * 128]
                qsl = self.YQ[:, hd * 512 + pr * 128:hd * 512 + (pr + 1) * 128]
                ksl = self.YQ[:, 2048 + hd * 512 + pr * 128:
                              2048 + hd * 512 + (pr + 1) * 128]
                ssl = self.S[:, (seq * 4 + hd) * 256:(seq * 4 + hd + 1) * 256]
                vbase = pr * 1024 + hd * 256
                self.wait("vector", "sS", self.gl_ready)
                self.v(lambda E, glsl=glsl: E.tensor_tensor_scan(
                    out=BP[:, 0:64], data0=glsl[:, 0:64], data1=glsl[:, 0:64],
                    initial=0.0, op0=ALU.add, op1=ALU.bypass))
                vbp = self.v(lambda E, glsl=glsl: E.tensor_tensor_scan(
                    out=BP[:, 64:128], data0=glsl[:, 64:128],
                    data1=glsl[:, 64:128],
                    initial=0.0, op0=ALU.add, op1=ALU.bypass))
                self.wait("scalar", "sV", vbp)
                self.s(lambda E: E.activation(out=EB, in_=BP, func=AF.Exp,
                                              scale=1.0 / 16))
                self.s(lambda E: E.activation(out=EM, in_=BP, func=AF.Exp,
                                              scale=-1.0 / 16))
                self.s(lambda E: E.activation(out=gA, in_=BP[:, 63:64],
                                              func=AF.Exp, scale=1.0 / 16))
                self.s(lambda E: E.activation(out=gB, in_=BP[:, 127:128],
                                              func=AF.Exp, scale=1.0 / 16))
                sgam = self.s(lambda E: E.activation(out=giA, in_=BP[:, 63:64],
                                                     func=AF.Exp,
                                                     scale=-1.0 / 16))
                self.wait("vector", "sS", sgam)
                self.v(lambda E: E.tensor_tensor(out=gAB, in0=gA, in1=gB,
                                                 op=ALU.mult))
                self.v(lambda E, qsl=qsl: E.scalar_tensor_tensor(
                    out=QEX[:, 0:64], in0=qsl[:, 0:64], scalar=SCALE_QK,
                    in1=EB[:, 0:64], op0=ALU.mult, op1=ALU.mult))
                self.v(lambda E, qsl=qsl: E.scalar_tensor_tensor(
                    out=QEX[:, 64:128], in0=qsl[:, 64:128], scalar=SCALE_QK,
                    in1=EB[:, 64:128], op0=ALU.mult, op1=ALU.mult))
                self.v(lambda E: E.tensor_scalar(
                    out=QEX[:, 64:128], in0=QEX[:, 64:128], scalar1=gA,
                    scalar2=None, op0=ALU.mult))
                self.v(lambda E, ksl=ksl: E.tensor_tensor(
                    out=KDX[:, 0:64], in0=ksl[:, 0:64], in1=EM[:, 0:64],
                    op=ALU.mult))
                self.v(lambda E, ksl=ksl: E.tensor_tensor(
                    out=KDX[:, 64:128], in0=ksl[:, 64:128], in1=EM[:, 64:128],
                    op=ALU.mult))
                self.v(lambda E: E.tensor_scalar(
                    out=KDX[:, 64:128], in0=KDX[:, 64:128], scalar1=giA,
                    scalar2=None, op0=ALU.mult))
                vprep = self.v(lambda E: E.tensor_scalar(
                    out=KADJ, in0=KDX, scalar1=gAB, scalar2=None, op0=ALU.mult))
                self.bank_acquire(0)
                self.wait("tensor", "sV", vprep)
                mmat = self.t(lambda E: E.matmul(
                    self.pb[0][:, 0:128], KDX, QEX, start=True, stop=True))
                self.wait("vector", "sT", mmat)
                vamp = self.v(lambda E: E.tensor_tensor(
                    out=self.AMP, in0=self.pb[0][:, 0:128],
                    in1=self.ONC[:, C_MASKP:C_MASKP + 128], op=ALU.mult))
                self.bank_done_evac(0, "sV")
                self.bank_acquire(1)
                self.wait("tensor", "sV", vamp)
                self.wait("tensor", "sV", self.s_last.get((seq, hd), 0))
                for ec in range(2):
                    self.t(lambda E, ec=ec, vbase=vbase: E.matmul(
                        self.pb[1][:, ec * 128:(ec + 1) * 128],
                        self.VV[:, vbase + ec * 128:vbase + (ec + 1) * 128],
                        self.AMP, start=(ec == 0), stop=False))
                for ec in range(2):
                    mmo = self.t(lambda E, ec=ec, ssl=ssl: E.matmul(
                        self.pb[1][:, ec * 128:(ec + 1) * 128],
                        ssl[:, ec * 128:(ec + 1) * 128],
                        QEX, start=False, stop=(ec == 1)))
                self.bank_acquire(2)
                mmt = self.t(lambda E: E.transpose(
                    self.pb[2][:, 0:128], KADJ,
                    self.ONC[:, C_IDN:C_IDN + 128]))
                self.wait("vector", "sT", mmt)
                vkt = self.v(lambda E: E.tensor_copy(out=KTs,
                                                     in_=self.pb[2][:, 0:128]))
                self.bank_done_evac(2, "sV")
                self.bank_acquire(3)
                self.wait("tensor", "sV", vkt)
                mmds = self.t(lambda E, vbase=vbase: E.matmul(
                    self.pb[3][:, 0:256], KTs,
                    self.VV[:, vbase:vbase + 256], start=True, stop=True))
                self.wait("vector", "sT", mmds)
                vS = self.v(lambda E, ssl=ssl: E.scalar_tensor_tensor(
                    out=ssl, in0=ssl, scalar=gAB, in1=self.pb[3][:, 0:256],
                    op0=ALU.mult, op1=ALU.add))
                self.bank_done_evac(3, "sV")
                self.s_last[(seq, hd)] = vS
                self.wait("scalar", "sT", mmo)
                for ec in range(2):
                    ssq = self.s(lambda E, ec=ec: E.activation(
                        out=self.SQ[:, ec * 128:(ec + 1) * 128],
                        in_=self.pb[1][:, ec * 128:(ec + 1) * 128],
                        func=AF.Square))
                self.bank_acquire(4)
                self.wait("tensor", "sS", ssq)
                for ec in range(2):
                    mmsq = self.t(lambda E, ec=ec: E.matmul(
                        self.pb[4][0:1, 0:128],
                        self.ONC[:, C_ONECOL:C_ONECOL + 1],
                        self.SQ[:, ec * 128:(ec + 1) * 128],
                        start=(ec == 0), stop=(ec == 1)))
                self.wait("scalar", "sT", mmsq)
                ssd = self.s(lambda E: E.activation(
                    out=sd_row, in_=self.pb[4][0:1, 0:128], func=AF.Sqrt,
                    scale=1.0 / 256,
                    bias=self.ONC[0:1, C_EPS_RMS:C_EPS_RMS + 1]))
                self.bank_done_evac(4, "sS")
                self.wait("vector", "sS", ssd)
                self.vdrain()
                vr = self.v(lambda E: E.reciprocal(out=r_row, in_=sd_row))
                self.vdrain()
                self.bank_acquire(5)
                self.wait("tensor", "sV", vr)
                mmbc = self.t(lambda E: E.matmul(
                    self.pb[5][:, 0:128],
                    self.ONC[0:1, C_ONEROW:C_ONEROW + 128],
                    r_row, start=True, stop=True))
                self.wait("vector", "sT", mmbc)
                self.v(lambda E: E.tensor_copy(out=self.RBC,
                                               in_=self.pb[5][:, 0:128]))
                self.bank_done_evac(5, "sV")
                for ec in range(2):
                    self.v(lambda E, ec=ec: E.scalar_tensor_tensor(
                        out=ONF, in0=self.pb[1][:, ec * 128:(ec + 1) * 128],
                        scalar=self.ccol(C_GNW + ec), in1=self.RBC,
                        op0=ALU.mult, op1=ALU.mult))
                    gslice = self.X0[:, (hd * 2 + ec) * 512 + pr * 128:
                                     (hd * 2 + ec) * 512 + (pr + 1) * 128]
                    vev = self.v(lambda E, ec=ec, hd=hd, pr=pr, gslice=gslice:
                                 E.tensor_tensor(
                        out=self.X0[:, (hd * 2 + ec) * 512 + pr * 128:
                                    (hd * 2 + ec) * 512 + (pr + 1) * 128],
                        in0=ONF, in1=gslice, op=ALU.mult))
                self.bank_done_evac(1, "sV")

    def emit_wo(self):
        self.wait("tensor", "sV", self.c["sV"])
        for f in range(NFC):
            slot, idx = self.wload(self.wd["wo_p"][f], 128, 1024)
            b = f % 2
            self.bank_acquire(b)
            self.wwait(idx)
            for kc in range(NFC):
                mm = self.t(lambda E, kc=kc, slot=slot, b=b: E.matmul(
                    self.pb[b], slot[:, kc * 128:(kc + 1) * 128],
                    self.chunk(self.X0, kc),
                    start=(kc == 0), stop=(kc == NFC - 1)))
            self.wretire()
            self.wait("vector", "sT", mm)
            self.v(lambda E, f=f, b=b: E.tensor_tensor(
                out=self.chunk(self.VV, f), in0=self.pb[b],
                in1=self.chunk(self.BB, f), op=ALU.add))
            self.bank_done_evac(b, "sV")

    def mlp_pt(self, m):
        if m < 8:
            return self.X0[:, m * 512:(m + 1) * 512]
        return self.PR[:, (m - 8) * 512:(m - 7) * 512]

    def emit_mlp(self):
        self.wait("tensor", "sV", self.c["sV"])
        for hf in range(2):
            for m in range(11):
                mi = hf * 11 + m
                sg, ig = self.wload(self.wd["wgate_p"][mi], 128, 1024)
                su, iu = self.wload(self.wd["wup_p"][mi], 128, 1024)
                bA, bB = (0, 1) if (m % 2 == 0) else (2, 3)
                self.bank_acquire(bA)
                self.wwait(ig)
                for kc in range(NFC):
                    mmA = self.t(lambda E, kc=kc, sg=sg, bA=bA: E.matmul(
                        self.pb[bA], sg[:, kc * 128:(kc + 1) * 128],
                        self.chunk(self.BB, kc),
                        start=(kc == 0), stop=(kc == NFC - 1)))
                self.wretire()
                self.bank_acquire(bB)
                self.wwait(iu)
                for kc in range(NFC):
                    mmB = self.t(lambda E, kc=kc, su=su, bB=bB: E.matmul(
                        self.pb[bB], su[:, kc * 128:(kc + 1) * 128],
                        self.chunk(self.BB, kc),
                        start=(kc == 0), stop=(kc == NFC - 1)))
                self.wretire()
                self.wait("scalar", "sT", mmA)
                ss = self.s(lambda E, bA=bA: E.activation(
                    out=self.SILU, in_=self.pb[bA], func=AF.Silu))
                self.bank_done_evac(bA, "sS")
                self.wait("vector", "sS", ss)
                self.wait("vector", "sT", mmB)
                vp = self.v(lambda E, m=m, bB=bB: E.tensor_tensor(
                    out=self.mlp_pt(m),
                    in0=self.pb[bB], in1=self.SILU, op=ALU.mult))
                self.bank_done_evac(bB, "sV")
            for f in range(NFC):
                p1, i1 = self.wload(
                    self.wd["wdown_p"][f][:, hf * 1408:hf * 1408 + 1024],
                    128, 1024)
                p2, i2 = self.wload(
                    self.wd["wdown_p"][f][:, hf * 1408 + 1024:hf * 1408 + 1408],
                    128, 384)
                bY = 6 if (f % 2 == 0) else 7
                self.bank_acquire(bY)
                self.wwait(i1)
                self.wait("tensor", "sV", vp)
                for kc in range(8):
                    self.t(lambda E, kc=kc, p1=p1, bY=bY: E.matmul(
                        self.pb[bY], p1[:, kc * 128:(kc + 1) * 128],
                        self.mlp_pt(kc),
                        start=(kc == 0), stop=False))
                self.wretire()
                self.wwait(i2)
                for kc in range(3):
                    mmY = self.t(lambda E, kc=kc, p2=p2, bY=bY: E.matmul(
                        self.pb[bY], p2[:, kc * 128:(kc + 1) * 128],
                        self.mlp_pt(8 + kc),
                        start=False, stop=(kc == 2)))
                self.wretire()
                self.wait("vector", "sT", mmY)
                if hf == 0:
                    self.v(lambda E, f=f, bY=bY: E.tensor_copy(
                        out=self.chunk(self.HX, f), in_=self.pb[bY]))
                else:
                    self.v(lambda E, f=f, bY=bY: E.tensor_tensor(
                        out=self.chunk(self.HX, f), in0=self.pb[bY],
                        in1=self.chunk(self.HX, f), op=ALU.add))
                    self.v(lambda E, f=f: E.tensor_tensor(
                        out=self.chunk(self.HX, f), in0=self.chunk(self.HX, f),
                        in1=self.chunk(self.VV, f), op=ALU.add))
                self.bank_done_evac(bY, "sV")

    def emit_add(self, a, b, dst):
        for c in range(NFC):
            vv = self.v(lambda E, c=c: E.tensor_tensor(
                out=self.chunk(dst, c), in0=self.chunk(a, c),
                in1=self.chunk(b, c), op=ALU.add))
        return vv

    def emit_tile(self, i, store_prev):
        p = self
        p.wait("vector", "sGS", store_prev)
        if i == 0:
            p.load_xtile(i, waits={"sV": p.c["sV"], "sT": p.c["sT"]})
            self.xg_pending = p.c["sGX"]
        xg = self.xg_pending
        for e in ("tensor", "scalar", "vector"):
            p.wait(e, "sGX", xg)
        p.emit_norm(p.X0, p.BB, "ln", eps_col=C_EPS_LN)
        if SPARSE_MOE:
            p.emit_moe_pre(p.BB, p.X1)
        p.emit_routing(p.BB)
        if SPARSE_MOE:
            p.emit_moe_sparse(p.BB, p.YQ, p.X1, p.VV)
        else:
            p.emit_moe(p.BB, p.YQ)
        p.emit_resid_half(p.X0, p.YQ, p.X1)
        p.emit_norm(p.X1, p.BB, "ln", eps_col=C_EPS_LN,
                    src_ready={"sV": p.c["sV"]})
        p.emit_norm(p.BB, p.HX, "rms", w_col0=C_ANW, eps_col=C_EPS_RMS,
                    src_ready={"sV": p.c["sV"]})
        p.emit_projections(p.HX)
        p.emit_gla()
        p.emit_wo()
        p.emit_norm(p.VV, p.BB, "rms", w_col0=C_MNW, eps_col=C_EPS_RMS,
                    src_ready={"sV": p.c["sV"]})
        p.emit_mlp()
        if i + 1 < self.ntiles:
            p.load_xtile(i + 1, waits={"sV": p.c["sV"], "sT": p.c["sT"]})
        self.xg_pending = p.c["sGX"]
        p.emit_add(p.HX, p.X1, p.HX)
        p.emit_norm(p.HX, p.BB, "ln", eps_col=C_EPS_LN,
                    src_ready={"sV": p.c["sV"]})
        if SPARSE_MOE:
            p.emit_moe_pre(p.BB, p.X1)
        p.emit_routing(p.BB)
        if SPARSE_MOE:
            p.emit_moe_sparse(p.BB, p.YQ, p.X1, p.VV)
        else:
            p.emit_moe(p.BB, p.YQ)
        p.emit_resid_half(p.HX, p.YQ, p.X1)
        return p.store_tile(i, p.X1)

    def tap(self, name, ap_src, shape):
        if name not in self.debug_taps:
            return
        ap = self.nc.dram_tensor("dbg_" + name, list(shape), F32,
                                 kind="ExternalOutput").ap()
        self.dbg[name] = ap
        self.gdma(ap, ap_src, sem="sGS",
                  waits={"sV": self.c["sV"], "sT": self.c["sT"],
                         "sS": self.c["sS"]})
        for e in ("vector", "scalar", "tensor"):
            self.wait(e, "sGS", self.c["sGS"])

    def finalize(self, store_total):
        nc = self.nc
        for ring in ([self.wring] + ([self.bring] if BF16_EXPERTS else [])):
            assert len(ring.retire) == ring.n, (ring.name, len(ring.retire), ring.n)
            assert sorted(ring.retire) == list(range(ring.n)), ring.name
        def emit_fill(E, ring, src, part, width, idx, gate):
            if idx >= ring.R:
                sname, cnt = ring.retire[idx - ring.R]
                if cnt > 0:
                    E.wait_ge(self.sem[sname], cnt)
            if gate is not None:
                for gn, gv in gate:
                    E.wait_ge(self.sem[gn], gv)
            E.dma_start(out=ring.slots[idx % ring.R][0:part, 0:width],
                        in_=src).then_inc(ring.sems[idx % ring.R], 16)

        def sync_body(E):
            for (ring, src, part, width, idx, gate) in self.fill_seq:
                if not (BF16_EXPERTS and ring is self.bring):
                    emit_fill(E, ring, src, part, width, idx, gate)
        with nc.Block() as block:
            @block.tensor
            def _(E):
                for fn in self.ops["tensor"]:
                    fn(E)
            @block.vector
            def _(E):
                for fn in self.ops["vector"]:
                    fn(E)
            @block.scalar
            def _(E):
                for fn in self.ops["scalar"]:
                    fn(E)
            @block.sync
            def _(E):
                sync_body(E)
            @block.gpsimd
            def _(E):
                for fn in self.ops["gpsimd"]:
                    fn(E)
                if store_total:
                    E.wait_ge(self.sem["sGS"], store_total)
        return nc

_CACHE = {}


def _build_full():
    if "nc" not in _CACHE:
        p = Prog(ntiles=NT)
        p.emit_preamble()
        if SPARSE_MOE and BF16_EXPERTS:
            p.emit_bf16_staging()
        total = 0
        for i in range(NT):
            total = p.emit_tile(i, total)
        nc = p.finalize(total)
        _CACHE["nc"] = nc
    return _CACHE["nc"]


def kernel(**inputs):
    nc = _build_full()
    x = np.asarray(inputs["L0_attn_hidden_states"], np.float32)
    consts = host_consts(inputs["L0_attn_attn_norm_w"],
                         inputs["L0_attn_mlp_norm_w"],
                         inputs["L0_attn_gnorm_w"],
                         inputs["L0_attn_bgk2"])
    w = pack_all_weights(inputs)
    in_maps = []
    for c in range(8):
        xc = np.ascontiguousarray(x[2 * c:2 * c + 2].transpose(0, 2, 1))
        in_maps.append({"x": xc, "consts": consts, **w})
    res = run_bass_kernel_spmd(nc, in_maps, core_ids=list(range(8)))
    outs = [np.asarray(res.results[c]["out"]).transpose(0, 2, 1)
            for c in range(8)]
    return np.ascontiguousarray(np.concatenate(outs, 0))


# version 11
# speedup vs baseline: 1.0630x; 1.0630x over previous
import numpy as np
import concourse.bass as bass
import concourse.mybir as mybir
from concourse.bass_utils import run_bass_kernel_spmd

F32 = mybir.dt.float32
AF = mybir.ActivationFunctionType
ALU = mybir.AluOpType
AX = mybir.AxisListType

NT = 8
TT = 512
POS = 256
DM = 1024
NFC = 8
SCALE_QK = float(128 ** -0.5)

C_IDN = 0
C_ONECOL = 128
C_ONEROW = 129
C_ANW = 257
C_MNW = 265
C_GNW = 273
C_BGK2 = 275
C_EPS_LN = 279
C_EPS_RMS = 280
C_MASKP = 281
C_IOTA256 = 409
C_IOTAJ = 665
C_NCOLS = 667


def host_consts(attn_norm_w, mlp_norm_w, gnorm_w, bgk2):
    c = np.zeros((128, C_NCOLS), np.float32)
    c[:, C_IDN:C_IDN + 128] = np.eye(128, dtype=np.float32)
    c[:, C_ONECOL] = 1.0
    c[0, C_ONEROW:C_ONEROW + 128] = 1.0
    c[:, C_ANW:C_ANW + 8] = np.asarray(attn_norm_w, np.float32).reshape(8, 128).T
    c[:, C_MNW:C_MNW + 8] = np.asarray(mlp_norm_w, np.float32).reshape(8, 128).T
    c[:, C_GNW:C_GNW + 2] = np.asarray(gnorm_w, np.float32).reshape(2, 128).T
    c[:, C_BGK2:C_BGK2 + 4] = np.asarray(bgk2, np.float32).reshape(4, 128).T
    c[:, C_EPS_LN] = 1e-5
    c[:, C_EPS_RMS] = 1e-6
    s, cc = np.mgrid[0:128, 0:128]
    mp = np.zeros((128, 128), np.float32)
    mp[(s < 64) & (cc < 64)] = (cc >= s)[(s < 64) & (cc < 64)]
    mp[(s < 64) & (cc >= 64)] = 1.0
    mp[(s >= 64) & (cc >= 64)] = (cc >= s)[(s >= 64) & (cc >= 64)]
    c[:, C_MASKP:C_MASKP + 128] = mp
    c[:, C_IOTA256:C_IOTA256 + 256] = np.arange(1, 257, dtype=np.float32)[None, :]
    c[:, C_IOTAJ] = np.arange(1, 129, dtype=np.float32)
    c[:, C_IOTAJ + 1] = np.arange(129, 257, dtype=np.float32)
    return c


def pack_lhsT(W):
    W = np.ascontiguousarray(np.asarray(W, np.float32))
    K, M = W.shape
    P = min(128, K)
    assert K % P == 0
    nk = K // P
    CH = min(128, M)
    assert M % CH == 0
    nm = M // CH
    out = np.empty((nm, P, nk * CH), np.float32)
    Wr = W.reshape(nk, P, M)
    for m in range(nm):
        blk = Wr[:, :, m * CH:(m + 1) * CH]
        out[m] = blk.transpose(1, 0, 2).reshape(P, nk * CH)
    return out


def pack_all_weights(ins):
    w = {}
    w["wq_p"] = pack_lhsT(ins["L0_attn_Wq"])
    w["wk_p"] = pack_lhsT(ins["L0_attn_Wk"])
    w["wgate_p"] = pack_lhsT(ins["L0_attn_W_gate"])
    w["wup_p"] = pack_lhsT(ins["L0_attn_W_up"])
    w["wdown_p"] = pack_lhsT(ins["L0_attn_W_down"])
    w["wo_p"] = pack_lhsT(ins["L0_attn_Wo"])
    w["wog_p"] = pack_lhsT(ins["L0_attn_Wg"])
    w["wgk1_p"] = pack_lhsT(ins["L0_attn_Wgk1"])[0]
    w["wgk2_p"] = pack_lhsT(ins["L0_attn_Wgk2"])
    w["wrouter_p"] = pack_lhsT(np.asarray(ins["L0_ffn_wg"], np.float32).T)[0]
    w["wv_nat"] = np.ascontiguousarray(np.asarray(ins["L0_attn_Wv"], np.float32))
    w1 = np.asarray(ins["L0_ffn_w1"], np.float32)
    w3 = np.asarray(ins["L0_ffn_w3"], np.float32)
    w2 = np.asarray(ins["L0_ffn_w2"], np.float32)
    w["moe1_p"] = np.stack([pack_lhsT(w1[e].T) for e in range(8)])
    w["moe3_p"] = np.stack([pack_lhsT(w3[e].T) for e in range(8)])
    w["moe2_p"] = np.stack([pack_lhsT(w2[e].T) for e in range(8)])
    w["moe2m_nat"] = np.ascontiguousarray(w2.transpose(0, 2, 1))
    return w


WEIGHT_SHAPES = {
    "wq_p": [4, 128, 1024], "wk_p": [4, 128, 1024],
    "wgate_p": [22, 128, 1024], "wup_p": [22, 128, 1024],
    "wdown_p": [8, 128, 2816], "wo_p": [8, 128, 1024], "wog_p": [8, 128, 1024],
    "wgk1_p": [128, 128], "wgk2_p": [4, 16, 128], "wrouter_p": [128, 64],
    "wv_nat": [1024, 1024],
    "moe1_p": [8, 16, 128, 1024], "moe3_p": [8, 16, 128, 1024],
    "moe2_p": [8, 8, 128, 2048], "moe2m_nat": [8, 2048, 1024],
}

ENGINES = ("tensor", "vector", "scalar", "sync", "gpsimd")
SPARSE_MOE = True
BF16_EXPERTS = True
BF = mybir.dt.bfloat16


class Ring:

    def __init__(self, prog, name, nslots, cols, dt):
        self.p = prog
        self.name = name
        self.R = nslots
        self.slots = [prog.nc.alloc_sbuf_tensor(f"{name}{j}", [128, cols], dt).ap()
                      for j in range(nslots)]
        self.sems = [prog.nc.alloc_semaphore(f"{name}s{j}") for j in range(nslots)]
        self.manifest = []
        self.n = 0
        self.retire = {}
        self._next_retire = 0

    def load(self, src_ap, part, width, gate=None):
        idx = self.n
        self.n += 1
        if getattr(self, "queue", "sync") == "gpsimd":
            ring = self
            def fill(E, src=src_ap, part=part, width=width, idx=idx, gate=gate):
                if idx >= ring.R:
                    sname, cnt = ring.retire[idx - ring.R]
                    if cnt > 0:
                        E.wait_ge(ring.p.sem[sname], cnt)
                if gate is not None:
                    for gn, gv in gate:
                        E.wait_ge(ring.p.sem[gn], gv)
                E.dma_start(out=ring.slots[idx % ring.R][0:part, 0:width],
                            in_=src).then_inc(ring.sems[idx % ring.R], 16)
            self.p.op("gpsimd", fill)
        else:
            self.p.fill_seq.append((self, src_ap, part, width, idx, gate))
        return self.slots[idx % self.R][0:part, 0:width], idx

    def wait_engine(self, eng, idx):
        sem = self.sems[idx % self.R]
        self.p.op(eng, lambda E, s=sem, v=16 * (idx // self.R + 1): E.wait_ge(s, v))

    def mark_retire(self, sem_name):
        while self._next_retire in self.retire:
            self._next_retire += 1
        self.retire[self._next_retire] = (sem_name, self.p.c[sem_name])
        self._next_retire += 1

    def mark_retire_idx(self, idx, sem_name):
        assert idx not in self.retire
        self.retire[idx] = (sem_name, self.p.c[sem_name])
CAP = 256
CJ0, CJ1 = 128, CAP - 128
RING_R = 6


class Prog:
    def __init__(self, ntiles=NT, debug_taps=()):
        self.nc = bass.Bass("TRN2", target_bir_lowering=False)
        nc = self.nc
        self.ntiles = ntiles
        self.debug_taps = set(debug_taps)
        self.ops = {e: [] for e in ENGINES}
        self.x = nc.dram_tensor("x", [2, DM, 2048], F32, kind="ExternalInput").ap()
        self.consts_d = nc.dram_tensor("consts", [128, C_NCOLS], F32,
                                       kind="ExternalInput").ap()
        self.wd = {k: nc.dram_tensor(k, v, F32, kind="ExternalInput").ap()
                   for k, v in WEIGHT_SHAPES.items()}
        self.out = nc.dram_tensor("out", [2, DM, 2048], F32, kind="ExternalOutput").ap()
        self.dbg = {}
        def sb(name, shape, dt=F32):
            return nc.alloc_sbuf_tensor(name, shape, dt).ap()
        self.X0 = sb("X0", [128, 4096])
        self.X1 = sb("X1", [128, 4096])
        self.BB = sb("BB", [128, 4096])
        self.YQ = sb("YQ", [128, 4096])
        self.VV = sb("VV", [128, 4096])
        self.HX = sb("HX", [128, 4096])
        self.PR = sb("PR", [128, 2048])
        self.S = sb("S", [128, 2048])
        self.SQ = sb("SQ", [128, 256])
        self.AMP = sb("AMP", [128, 128])
        self.GS = sb("GS", [128, 1288])
        self.EW = sb("EWs", [128, 52])
        self.WBC = sb("WBC", [128, 512])
        self.SILU = sb("SILU", [128, 512])
        self.SILU2 = sb("SILU2", [128, 512])
        self.EWF = sb("EWF", [8, 512])
        self.EWS8a = sb("EWS8a", [8, 512])
        self.EWS8b = sb("EWS8b", [8, 512])
        self.RNKT = sb("RNKT", [128, 32])
        self.U16 = sb("U16", [16, 512])
        self.ROWS = sb("ROWS", [1, 1024])
        self.RBC = sb("RBC", [128, 128])
        self.RNKBC = sb("RNKBC", [128, 512])
        self.ONC = sb("ONC", [128, C_NCOLS])
        self.WSMALL = sb("WSMALL", [128, 704])
        self.pb = [nc.alloc_psum_tensor(f"pb{j}", [128, 512], F32).ap()
                   for j in range(8)]
        self.bank_free = [("V", 0)] * 8
        names = ["sT", "sV", "sS", "sGC", "sGX", "sGS", "sGB0", "sGB1"]
        self.sem = {n: nc.alloc_semaphore(n) for n in names}
        self.c = {n: 0 for n in self.sem}
        self.fill_seq = []
        self.wring = Ring(self, "wr", 4, 1024, F32)
        if BF16_EXPERTS:
            self.bring = Ring(self, "wb", 18, 1024, BF)
            self.bring.queue = "gpsimd"
            self.GHB = nc.alloc_sbuf_tensor("GHB", [128, 2048], BF).ap()
            self.GLB = nc.alloc_sbuf_tensor("GLB", [128, 2048], BF).ap()
            self.PHB = nc.alloc_sbuf_tensor("PHB", [128, 512], BF).ap()
            self.PLB = nc.alloc_sbuf_tensor("PLB", [128, 512], BF).ap()
            self.w13h = nc.dram_tensor("w13h", [8, 2, 16, 128, 1024], BF).ap()
            self.w13l = nc.dram_tensor("w13l", [8, 2, 16, 128, 1024], BF).ap()
            self.w2h = nc.dram_tensor("w2h", [8, 16, 128, 1024], BF).ap()
            self.w2l = nc.dram_tensor("w2l", [8, 16, 128, 1024], BF).ap()

    def op(self, eng, fn):
        self.ops[eng].append(fn)

    def wait(self, eng, sem, val):
        if val <= 0:
            return
        s = self.sem[sem]
        self.op(eng, lambda E, s=s, v=val: E.wait_ge(s, v))

    def _inc(self, eng, fn, sem, n=1):
        s = self.sem[sem]
        self.op(eng, lambda E, fn=fn, s=s, n=n: fn(E).then_inc(s, n))
        self.c[sem] += n
        return self.c[sem]

    def t(self, fn):
        return self._inc("tensor", fn, "sT")

    def v(self, fn):
        return self._inc("vector", fn, "sV")

    def s(self, fn):
        return self._inc("scalar", fn, "sS")

    def plain(self, eng, fn):
        self.op(eng, fn)

    def vdrain(self):
        self.op("vector", lambda E: E.drain())

    def wait_counts(self, eng, counts):
        for k, v in counts.items():
            self.wait(eng, k, v)

    def barrier_others(self, eng):
        m = {"tensor": "sT", "vector": "sV", "scalar": "sS"}
        for e2, s2 in m.items():
            if e2 != eng:
                self.wait(eng, s2, self.c[s2])

    def bank_acquire(self, j):
        sem, cnt = self.bank_free[j]
        self.wait("tensor", sem, cnt)

    def bank_done_evac(self, j, sem):
        self.bank_free[j] = (sem, self.c[sem])

    def wload(self, src_ap, part, width):
        return self.wring.load(src_ap, part, width)

    def wwait(self, idx):
        self.wring.wait_engine("tensor", idx)

    def wretire(self):
        self.wring.mark_retire("sT")

    def gdma(self, out_ap, in_ap, waits=None, sem="sGX"):
        s = self.sem[sem]
        def fn(E, out_ap=out_ap, in_ap=in_ap, waits=waits or {}):
            for k, v in waits.items():
                E.wait_ge(self.sem[k], v)
            E.dma_start(out=out_ap, in_=in_ap).then_inc(s, 16)
        self.op("gpsimd", fn)
        self.c[sem] += 16
        return self.c[sem]

    def chunk(self, buf, c, width=512):
        return buf[:, c * width:(c + 1) * width]

    def ccol(self, col, p0=0, p1=128):
        return self.ONC[p0:p1, col:col + 1]

    def emit_preamble(self):
        self.gdma(self.ONC[:, 0:C_NCOLS], self.consts_d, sem="sGC")
        self.gdma(self.WSMALL[:, 0:64], self.wd["wrouter_p"], sem="sGC")
        self.gdma(self.WSMALL[:, 64:192], self.wd["wgk1_p"], sem="sGC")
        for m in range(4):
            self.gdma(self.WSMALL[0:16, 192 + m * 128:192 + (m + 1) * 128],
                      self.wd["wgk2_p"][m], sem="sGC")
        g = self.c["sGC"]
        self.g_consts = g
        self.v(lambda E: E.memset(self.S, 0.0))
        self.v(lambda E: E.memset(self.AMP, 0.0))
        for e in ("tensor", "vector", "scalar"):
            self.wait(e, "sGC", self.g_consts)

    def load_xtile(self, i, waits=None):
        last = 0
        for fc in range(NFC):
            for sq in range(2):
                dst = self.X0[:, fc * 512 + sq * POS: fc * 512 + (sq + 1) * POS]
                src = self.x[sq, fc * 128:(fc + 1) * 128, i * POS:(i + 1) * POS]
                last = self.gdma(dst, src, sem="sGX",
                                 waits=waits if (fc == 0 and sq == 0) else None)
        return last

    def store_tile(self, i, srcbuf):
        last = 0
        w = {"sV": self.c["sV"]}
        for fc in range(NFC):
            for sq in range(2):
                src = srcbuf[:, fc * 512 + sq * POS: fc * 512 + (sq + 1) * POS]
                dst = self.out[sq, fc * 128:(fc + 1) * 128, i * POS:(i + 1) * POS]
                last = self.gdma(dst, src, sem="sGS",
                                 waits=w if (fc == 0 and sq == 0) else None)
        return last

    def emit_norm(self, src, dst, kind, w_col0=None, eps_col=C_EPS_RMS,
                  src_ready=None):
        rows = self.ROWS
        m_row = rows[0:1, 0:512]; r_row = rows[0:1, 512:1024]
        t1 = self.SILU[0:1, 0:512]; t2 = self.SILU2[0:1, 0:512]
        if src_ready:
            self.wait_counts("tensor", src_ready)
            self.wait_counts("scalar", src_ready)
        bsum, bsq, bb1, bb2 = 0, 1, 2, 3
        sq_cnt = []
        for c in range(NFC):
            sq_cnt.append(self.s(lambda E, c=c: E.activation(
                out=self.chunk(dst, c), in_=self.chunk(src, c), func=AF.Square)))
        if kind == "ln":
            self.bank_acquire(bsum)
            for c in range(NFC):
                cnt = self.t(lambda E, c=c: E.matmul(
                    self.pb[bsum][0:1, :], self.ONC[:, C_ONECOL:C_ONECOL + 1],
                    self.chunk(src, c), start=(c == 0), stop=(c == NFC - 1)))
            mm_sum_end = cnt
        self.bank_acquire(bsq)
        for c in range(NFC):
            self.wait("tensor", "sS", sq_cnt[c])
            cnt = self.t(lambda E, c=c: E.matmul(
                self.pb[bsq][0:1, :], self.ONC[:, C_ONECOL:C_ONECOL + 1],
                self.chunk(dst, c), start=(c == 0), stop=(c == NFC - 1)))
        mm_sq_end = cnt
        if kind == "ln":
            self.wait("scalar", "sT", mm_sum_end)
            self.s(lambda E: E.activation(out=m_row, in_=self.pb[bsum][0:1, :],
                                          func=AF.Copy, scale=1.0 / DM))
        self.wait("scalar", "sT", mm_sq_end)
        s_msq = self.s(lambda E: E.activation(out=t1, in_=self.pb[bsq][0:1, :],
                                              func=AF.Copy, scale=1.0 / DM))
        self.bank_done_evac(bsq, "sS")
        self.wait("vector", "sS", s_msq)
        if kind == "ln":
            self.bank_done_evac(bsum, "sS")
            self.v(lambda E: E.tensor_tensor(out=t2, in0=m_row, in1=m_row,
                                             op=ALU.mult))
            vv = self.v(lambda E: E.tensor_tensor(out=t1, in0=t1, in1=t2,
                                                  op=ALU.subtract))
            self.wait("scalar", "sV", vv)
        s_sd = self.s(lambda E: E.activation(
            out=t2, in_=t1, func=AF.Sqrt,
            bias=self.ONC[0:1, eps_col:eps_col + 1]))
        self.wait("vector", "sS", s_sd)
        self.vdrain()
        v_r = self.v(lambda E: E.reciprocal(out=r_row, in_=t2))
        self.vdrain()
        self.wait("tensor", "sV", v_r)
        if kind == "ln":
            self.bank_acquire(bb1)
            self.t(lambda E: E.matmul(self.pb[bb1], self.ONC[0:1, C_ONEROW:C_ONEROW + 128],
                                      m_row, start=True, stop=True))
        self.bank_acquire(bb2)
        mm_bc = self.t(lambda E: E.matmul(self.pb[bb2],
                                          self.ONC[0:1, C_ONEROW:C_ONEROW + 128],
                                          r_row, start=True, stop=True))
        self.wait("vector", "sT", mm_bc)
        for c in range(NFC):
            if kind == "ln":
                self.v(lambda E, c=c: E.tensor_tensor(
                    out=self.chunk(dst, c), in0=self.chunk(src, c),
                    in1=self.pb[bb1], op=ALU.subtract))
                vv = self.v(lambda E, c=c: E.tensor_tensor(
                    out=self.chunk(dst, c), in0=self.chunk(dst, c),
                    in1=self.pb[bb2], op=ALU.mult))
            else:
                vv = self.v(lambda E, c=c: E.scalar_tensor_tensor(
                    out=self.chunk(dst, c), in0=self.chunk(src, c),
                    scalar=self.ccol(w_col0 + c), in1=self.pb[bb2],
                    op0=ALU.mult, op1=ALU.mult))
        if kind == "ln":
            self.bank_done_evac(bb1, "sV")
        self.bank_done_evac(bb2, "sV")
        return vv

    def emit_routing(self, src):
        self.wait("tensor", "sV", self.c["sV"])
        slot = self.WSMALL[:, 0:64]
        EW = self.EW
        LGs = EW[:, 0:8]; m8 = EW[:, 8:16]; mr = EW[:, 16:24]
        repl = EW[:, 24:32]; sel = EW[:, 32:40]; eraw = EW[:, 40:48]
        nl1 = EW[:, 48:49]; den = EW[:, 49:50]; rec = EW[:, 50:51]
        e2c = EW[:, 51:52]
        for g in range(4):
            blg, btr = 5, 4
            self.bank_acquire(blg)
            for kc in range(NFC):
                mm_lg = self.t(lambda E, g=g, kc=kc, slot=slot: E.matmul(
                    self.pb[blg][:, 0:8],
                    src[:, kc * 512 + g * 128: kc * 512 + (g + 1) * 128],
                    slot[:, kc * 8:(kc + 1) * 8],
                    start=(kc == 0), stop=(kc == NFC - 1)))
            self.wait("vector", "sT", mm_lg)
            self.v(lambda E: E.tensor_copy(out=LGs, in_=self.pb[blg][:, 0:8]))
            self.bank_done_evac(blg, "sV")
            self.vdrain()
            self.v(lambda E: E.max(out=m8, in_=LGs))
            self.vdrain()
            self.v(lambda E: E.memset(mr, -1e30))
            self.v(lambda E: E.tensor_copy(out=mr[:, 0:2], in_=m8[:, 0:2]))
            self.vdrain()
            self.v(lambda E: E.match_replace(out=repl, in_to_replace=mr,
                                             in_values=LGs, imm_value=-1e30))
            self.vdrain()
            self.v(lambda E: E.tensor_tensor(out=sel, in0=LGs, in1=repl,
                                             op=ALU.not_equal))
            vnl = self.v(lambda E: E.tensor_scalar(out=nl1, in0=m8[:, 0:1],
                                                   scalar1=-1.0, scalar2=None,
                                                   op0=ALU.mult))
            self.wait("scalar", "sV", vnl)
            self.s(lambda E: E.activation(out=eraw, in_=LGs, func=AF.Exp,
                                          bias=nl1))
            se2 = self.s(lambda E: E.activation(out=e2c, in_=m8[:, 1:2],
                                                func=AF.Exp, bias=nl1))
            self.wait("vector", "sS", se2)
            self.v(lambda E: E.tensor_scalar(out=den, in0=e2c, scalar1=1.0,
                                             scalar2=None, op0=ALU.add))
            self.vdrain()
            self.v(lambda E: E.reciprocal(out=rec, in_=den))
            self.vdrain()
            self.v(lambda E: E.tensor_tensor(out=eraw, in0=eraw, in1=sel,
                                             op=ALU.mult))
            vew = self.v(lambda E: E.tensor_scalar(out=eraw, in0=eraw,
                                                   scalar1=rec, scalar2=None,
                                                   op0=ALU.mult))
            self.bank_acquire(btr)
            self.wait("tensor", "sV", vew)
            mmt = self.t(lambda E: E.transpose(
                self.pb[btr][0:8, 0:128], eraw,
                self.ONC[:, C_IDN:C_IDN + 128]))
            self.wait("vector", "sT", mmt)
            self.v(lambda E, g=g: E.tensor_copy(
                out=self.EWF[:, g * 128:(g + 1) * 128],
                in_=self.pb[btr][0:8, 0:128]))
            self.bank_done_evac(btr, "sV")

    def emit_moe(self, src, dst):
        self.wait("tensor", "sV", self.c["sV"])
        ewrow = self.U16[0:1, 0:512]
        first = True
        for e in range(8):
            brow, bbc = 4, 5
            self.bank_acquire(brow)
            m1 = self.t(lambda E, e=e: E.matmul(
                self.pb[brow][0:1, 0:512],
                self.ONC[0:8, C_IDN + e:C_IDN + e + 1],
                self.EWF, start=True, stop=True))
            self.wait("scalar", "sT", m1)
            sc = self.s(lambda E: E.activation(out=ewrow, in_=self.pb[brow][0:1, 0:512],
                                               func=AF.Copy))
            self.bank_done_evac(brow, "sS")
            self.bank_acquire(bbc)
            self.wait("tensor", "sS", sc)
            m2 = self.t(lambda E: E.matmul(
                self.pb[bbc], self.ONC[0:1, C_ONEROW:C_ONEROW + 128],
                ewrow, start=True, stop=True))
            self.wait("vector", "sT", m2)
            vbc = self.v(lambda E: E.tensor_copy(out=self.WBC, in_=self.pb[bbc]))
            self.bank_done_evac(bbc, "sV")
            for hf in range(2):
                for m in range(8):
                    mi = hf * 8 + m
                    s1, i1 = self.wload(self.wd["moe1_p"][e, mi], 128, 1024)
                    s3, i3 = self.wload(self.wd["moe3_p"][e, mi], 128, 1024)
                    bA, bB = (0, 1) if (m % 2 == 0) else (2, 3)
                    self.bank_acquire(bA)
                    self.wwait(i1)
                    for kc in range(NFC):
                        mmA = self.t(lambda E, kc=kc, s1=s1, bA=bA: E.matmul(
                            self.pb[bA], s1[:, kc * 128:(kc + 1) * 128],
                            self.chunk(src, kc),
                            start=(kc == 0), stop=(kc == NFC - 1)))
                    self.wretire()
                    self.bank_acquire(bB)
                    self.wwait(i3)
                    for kc in range(NFC):
                        mmB = self.t(lambda E, kc=kc, s3=s3, bB=bB: E.matmul(
                            self.pb[bB], s3[:, kc * 128:(kc + 1) * 128],
                            self.chunk(src, kc),
                            start=(kc == 0), stop=(kc == NFC - 1)))
                    self.wretire()
                    self.wait("scalar", "sT", mmA)
                    ss = self.s(lambda E, bA=bA: E.activation(
                        out=self.SILU, in_=self.pb[bA], func=AF.Silu))
                    self.bank_done_evac(bA, "sS")
                    self.wait("vector", "sS", ss)
                    self.wait("vector", "sV", vbc)
                    self.v(lambda E: E.tensor_tensor(out=self.SILU2, in0=self.SILU,
                                                     in1=self.WBC, op=ALU.mult))
                    self.wait("vector", "sT", mmB)
                    vp = self.v(lambda E, m=m, bB=bB: E.tensor_tensor(
                        out=self.PR[:, m * 512:(m + 1) * 512],
                        in0=self.pb[bB], in1=self.SILU2, op=ALU.mult))
                    self.bank_done_evac(bB, "sV")
                for f in range(NFC):
                    s2, i2 = self.wload(
                        self.wd["moe2_p"][e, f][:, hf * 1024:(hf + 1) * 1024],
                        128, 1024)
                    bY = 6 if (f % 2 == 0) else 7
                    self.bank_acquire(bY)
                    self.wwait(i2)
                    self.wait("tensor", "sV", vp)
                    for kc in range(NFC):
                        mmY = self.t(lambda E, kc=kc, s2=s2, bY=bY: E.matmul(
                            self.pb[bY], s2[:, kc * 128:(kc + 1) * 128],
                            self.PR[:, kc * 512:(kc + 1) * 512],
                            start=(kc == 0), stop=(kc == NFC - 1)))
                    self.wretire()
                    self.wait("vector", "sT", mmY)
                    if first:
                        self.v(lambda E, f=f, bY=bY: E.tensor_copy(
                            out=self.chunk(dst, f), in_=self.pb[bY]))
                    else:
                        self.v(lambda E, f=f, bY=bY: E.tensor_tensor(
                            out=self.chunk(dst, f), in0=self.pb[bY],
                            in1=self.chunk(dst, f), op=ALU.add))
                    self.bank_done_evac(bY, "sV")
                first = False

    def emit_bf16_staging(self):
        jobs = []
        for e in range(8):
            for mat in range(2):
                nm = "moe1_p" if mat == 0 else "moe3_p"
                for m in range(16):
                    jobs.append((self.wd[nm][e, m],
                                 self.w13h[e, mat, m], self.w13l[e, mat, m]))
        for e in range(8):
            for kc in range(16):
                jobs.append((self.wd["moe2m_nat"][e, kc * 128:(kc + 1) * 128, :],
                             self.w2h[e, kc], self.w2l[e, kc]))
        for j, (src, dh, dl) in enumerate(jobs):
            slot, idx = self.wring.load(src, 128, 1024)
            self.wring.wait_engine("vector", idx)
            pp = j % 2
            sem = "sGB0" if pp == 0 else "sGB1"
            hi = self.GHB[:, pp * 1024:(pp + 1) * 1024]
            lo = self.GLB[:, pp * 1024:(pp + 1) * 1024]
            if j >= 2:
                self.wait("vector", sem, 32 * (j // 2))
            self.v(lambda E, hi=hi, slot=slot: E.tensor_copy(out=hi, in_=slot))
            vv = self.v(lambda E, hi=hi, lo=lo, slot=slot: E.tensor_tensor(
                out=lo, in0=slot, in1=hi, op=ALU.subtract))
            self.wring.mark_retire("sV")
            self.gdma(dh, hi, waits={"sV": vv}, sem=sem)
            self.gdma(dl, lo, sem=sem)
        self.stage_gate = [("sGB0", self.c["sGB0"]), ("sGB1", self.c["sGB1"])]
        self.first_bload = True

    def bload(self, src, part=128, width=1024):
        gate = self.stage_gate if self.first_bload else None
        self.first_bload = False
        return self.bring.load(src, part, width, gate=gate)

    def emit_moe_pre(self, src, ytm):
        self.wait("tensor", "sV", self.c["sV"])
        for g in range(4):
            for fc in range(NFC):
                b = 6 + (fc % 2)
                self.bank_acquire(b)
                mmt = self.t(lambda E, g=g, fc=fc, b=b: E.transpose(
                    self.pb[b][:, 0:128],
                    src[:, fc * 512 + g * 128:fc * 512 + (g + 1) * 128],
                    self.ONC[:, C_IDN:C_IDN + 128]))
                self.wait("vector", "sT", mmt)
                self.v(lambda E, g=g, fc=fc, b=b: E.tensor_copy(
                    out=ytm[:, g * 1024 + fc * 128:g * 1024 + (fc + 1) * 128],
                    in_=self.pb[b][:, 0:128]))
                self.bank_done_evac(b, "sV")

    def emit_moe_sparse(self, src, dst, ytm, gye):
        self.wait("tensor", "sV", self.c["sV"])
        ewrow = self.U16[0:1, 0:512]
        SEL = self.EW[:, 0:8]
        SELF_ = self.nc
        selF = self.EWS8a
        rankF = self.EWS8b
        self.v(lambda E: E.tensor_scalar(out=selF, in0=self.EWF, scalar1=0.0,
                                         scalar2=None, op0=ALU.not_equal))
        self.v(lambda E: E.tensor_tensor_scan(out=rankF, data0=selF, data1=selF,
                                              initial=0.0, op0=ALU.add,
                                              op1=ALU.bypass))
        vrk = self.v(lambda E: E.tensor_tensor(out=rankF, in0=rankF, in1=selF,
                                               op=ALU.mult))
        for g in range(4):
            self.bank_acquire(6)
            self.wait("tensor", "sV", vrk)
            mmt = self.t(lambda E, g=g: E.transpose(
                self.pb[6][0:128, 0:8], rankF[:, g * 128:(g + 1) * 128],
                self.ONC[0:8, C_IDN:C_IDN + 8]))
            self.wait("vector", "sT", mmt)
            self.v(lambda E, g=g: E.tensor_copy(
                out=self.RNKT[:, g * 8:(g + 1) * 8], in_=self.pb[6][0:128, 0:8]))
            self.bank_done_evac(6, "sV")
        GFM = gye
        first = True
        for e in range(8):
            for srcrow, dstbc in ((self.EWF, self.WBC), (rankF, self.RNKBC)):
                brow, bbc = 4, 5
                self.bank_acquire(brow)
                m1 = self.t(lambda E, e=e, srcrow=srcrow: E.matmul(
                    self.pb[brow][0:1, 0:512],
                    self.ONC[0:8, C_IDN + e:C_IDN + e + 1],
                    srcrow, start=True, stop=True))
                self.wait("scalar", "sT", m1)
                sc = self.s(lambda E: E.activation(
                    out=ewrow, in_=self.pb[brow][0:1, 0:512], func=AF.Copy))
                self.bank_done_evac(brow, "sS")
                self.bank_acquire(bbc)
                self.wait("tensor", "sS", sc)
                m2 = self.t(lambda E: E.matmul(
                    self.pb[bbc], self.ONC[0:1, C_ONEROW:C_ONEROW + 128],
                    ewrow, start=True, stop=True))
                self.wait("vector", "sT", m2)
                self.v(lambda E, dstbc=dstbc: E.tensor_copy(out=dstbc,
                                                            in_=self.pb[bbc]))
                self.bank_done_evac(bbc, "sV")
            for g in range(4):
                vpb = self.v(lambda E, g=g, e=e: E.tensor_scalar(
                    out=self.PR[:, g * 256:g * 256 + CAP],
                    in0=self.ONC[:, C_IOTA256:C_IOTA256 + CAP],
                    scalar1=self.RNKT[:, g * 8 + e:g * 8 + e + 1],
                    scalar2=None, op0=ALU.is_equal))
            for fc in range(NFC):
                b = fc % 2
                self.bank_acquire(b)
                self.wait("tensor", "sV", vpb)
                for g in range(4):
                    mmg = self.t(lambda E, g=g, fc=fc, b=b: E.matmul(
                        self.pb[b][:, 0:CAP],
                        ytm[:, g * 1024 + fc * 128:g * 1024 + (fc + 1) * 128],
                        self.PR[:, g * 256:g * 256 + CAP],
                        start=(g == 0), stop=(g == 3)))
                self.wait("vector", "sT", mmg)
                if BF16_EXPERTS:
                    self.v(lambda E, fc=fc, b=b: E.tensor_copy(
                        out=self.GHB[:, fc * 256:fc * 256 + CAP],
                        in_=self.pb[b][:, 0:CAP]))
                    self.v(lambda E, fc=fc, b=b: E.tensor_tensor(
                        out=self.GLB[:, fc * 256:fc * 256 + CAP],
                        in0=self.pb[b][:, 0:CAP],
                        in1=self.GHB[:, fc * 256:fc * 256 + CAP],
                        op=ALU.subtract))
                else:
                    self.v(lambda E, fc=fc, b=b: E.tensor_copy(
                        out=GFM[:, fc * 256:fc * 256 + CAP],
                        in_=self.pb[b][:, 0:CAP]))
                self.bank_done_evac(b, "sV")
            if BF16_EXPERTS:
                for b4 in (4, 5, 6, 7):
                    self.bank_acquire(b4)
                prev = None
                for m in range(17):
                    if m < 16:
                        w1h, i1h = self.bload(self.w13h[e, 0, m])
                        w1l, i1l = self.bload(self.w13l[e, 0, m])
                        w3h, i3h = self.bload(self.w13h[e, 1, m])
                        w3l, i3l = self.bload(self.w13l[e, 1, m])
                        w2hs, i2h = self.bload(self.w2h[e, m])
                        w2ls, i2l = self.bload(self.w2l[e, m])
                        bA, bB = (0, 1) if (m % 2 == 0) else (2, 3)
                        self.bank_acquire(bA)
                        self.bring.wait_engine("tensor", i1h)
                        self.bring.wait_engine("tensor", i1l)
                        np_ = 0
                        for kc in range(NFC):
                            for wsl, gsl in ((w1h, self.GHB), (w1h, self.GLB),
                                             (w1l, self.GHB)):
                                mmA = self.t(lambda E, kc=kc, wsl=wsl, gsl=gsl,
                                             bA=bA, np_=np_: E.matmul(
                                    self.pb[bA][:, 0:CAP],
                                    wsl[:, kc * 128:(kc + 1) * 128],
                                    gsl[:, kc * 256:kc * 256 + CAP],
                                    start=(np_ == 0), stop=(np_ == 23)))
                                np_ += 1
                        self.bring.mark_retire_idx(i1h, "sT")
                        self.bring.mark_retire_idx(i1l, "sT")
                        self.bank_acquire(bB)
                        self.bring.wait_engine("tensor", i3h)
                        self.bring.wait_engine("tensor", i3l)
                        np_ = 0
                        for kc in range(NFC):
                            for wsl, gsl in ((w3h, self.GHB), (w3h, self.GLB),
                                             (w3l, self.GHB)):
                                mmB = self.t(lambda E, kc=kc, wsl=wsl, gsl=gsl,
                                             bB=bB, np_=np_: E.matmul(
                                    self.pb[bB][:, 0:CAP],
                                    wsl[:, kc * 128:(kc + 1) * 128],
                                    gsl[:, kc * 256:kc * 256 + CAP],
                                    start=(np_ == 0), stop=(np_ == 23)))
                                np_ += 1
                        self.bring.mark_retire_idx(i3h, "sT")
                        self.bring.mark_retire_idx(i3l, "sT")
                    if prev is not None:
                        pm, pw2h, pw2l, pi2h, pi2l, pvph, pp = prev
                        self.wait("tensor", "sV", pvph)
                        self.bring.wait_engine("tensor", pi2h)
                        self.bring.wait_engine("tensor", pi2l)
                        for jm in range(2):
                            jw = CJ0 if jm == 0 else CJ1
                            for fh in range(2):
                                b = 4 + jm * 2 + fh
                                np_ = 0
                                for psl, wsl in ((self.PHB, pw2h),
                                                 (self.PHB, pw2l),
                                                 (self.PLB, pw2h)):
                                    mmY = self.t(lambda E, jm=jm, fh=fh, b=b,
                                                 jw=jw, psl=psl, wsl=wsl,
                                                 pm=pm, pp=pp,
                                                 np_=np_: E.matmul(
                                        self.pb[b][0:jw, :],
                                        psl[:, pp * 256 + jm * 128:
                                            pp * 256 + jm * 128 + jw],
                                        wsl[:, fh * 512:(fh + 1) * 512],
                                        start=(pm == 0 and np_ == 0),
                                        stop=(pm == 15 and np_ == 2)))
                                    np_ += 1
                        self.bring.mark_retire_idx(pi2h, "sT")
                        self.bring.mark_retire_idx(pi2l, "sT")
                    if m < 16:
                        pp = m % 2
                        self.wait("scalar", "sT", mmA)
                        ss = self.s(lambda E, bA=bA: E.activation(
                            out=self.SILU[:, 0:CAP], in_=self.pb[bA][:, 0:CAP],
                            func=AF.Silu))
                        self.bank_done_evac(bA, "sS")
                        self.wait("vector", "sS", ss)
                        self.wait("vector", "sT", mmB)
                        self.v(lambda E, bB=bB: E.tensor_tensor(
                            out=self.SILU2[:, 0:CAP], in0=self.pb[bB][:, 0:CAP],
                            in1=self.SILU[:, 0:CAP], op=ALU.mult))
                        self.bank_done_evac(bB, "sV")
                        self.v(lambda E, pp=pp: E.tensor_copy(
                            out=self.PHB[:, pp * 256:pp * 256 + CAP],
                            in_=self.SILU2[:, 0:CAP]))
                        vph = self.v(lambda E, pp=pp: E.tensor_tensor(
                            out=self.PLB[:, pp * 256:pp * 256 + CAP],
                            in0=self.SILU2[:, 0:CAP],
                            in1=self.PHB[:, pp * 256:pp * 256 + CAP],
                            op=ALU.subtract))
                        prev = (m, w2hs, w2ls, i2h, i2l, vph, pp)
                self.wait("vector", "sT", self.c["sT"])
                for jm in range(2):
                    jw = CJ0 if jm == 0 else CJ1
                    for fh in range(2):
                        b = 4 + jm * 2 + fh
                        self.v(lambda E, jm=jm, fh=fh, b=b, jw=jw: E.tensor_copy(
                            out=gye[0:jw, 2048 + jm * 1024 + fh * 512:
                                    2048 + jm * 1024 + (fh + 1) * 512],
                            in_=self.pb[b][0:jw, :]))
                        self.bank_done_evac(b, "sV")
            else:
                raise NotImplementedError("fp32 sparse experts removed; "
                                          "use dense emit_moe instead")
            for jc in range(2):
                self.v(lambda E, jc=jc: E.tensor_scalar(
                    out=self.PR[:, 1024 + jc * 512:1024 + (jc + 1) * 512],
                    in0=self.RNKBC, scalar1=self.ccol(C_IOTAJ + jc),
                    scalar2=None, op0=ALU.is_equal))
                vptw = self.v(lambda E, jc=jc: E.tensor_tensor(
                    out=self.PR[:, 1024 + jc * 512:1024 + (jc + 1) * 512],
                    in0=self.PR[:, 1024 + jc * 512:1024 + (jc + 1) * 512],
                    in1=self.WBC, op=ALU.mult))
            for fc in range(NFC):
                b = 4 + (fc % 2)
                self.bank_acquire(b)
                self.wait("tensor", "sV", vptw)
                for jc in range(2):
                    jw = CJ0 if jc == 0 else CJ1
                    mms = self.t(lambda E, jc=jc, fc=fc, b=b, jw=jw: E.matmul(
                        self.pb[b],
                        gye[0:jw, 2048 + jc * 1024 + fc * 128:
                            2048 + jc * 1024 + (fc + 1) * 128],
                        self.PR[0:jw, 1024 + jc * 512:1024 + (jc + 1) * 512],
                        start=(jc == 0), stop=(jc == 1)))
                self.wait("vector", "sT", mms)
                if first:
                    self.v(lambda E, fc=fc, b=b: E.tensor_copy(
                        out=self.chunk(dst, fc), in_=self.pb[b]))
                else:
                    self.v(lambda E, fc=fc, b=b: E.tensor_tensor(
                        out=self.chunk(dst, fc), in0=self.pb[b],
                        in1=self.chunk(dst, fc), op=ALU.add))
                self.bank_done_evac(b, "sV")
            first = False

    def emit_resid_half(self, base, delta, dst):
        for c in range(NFC):
            vv = self.v(lambda E, c=c: E.scalar_tensor_tensor(
                out=self.chunk(dst, c), in0=self.chunk(delta, c), scalar=0.5,
                in1=self.chunk(base, c), op0=ALU.mult, op1=ALU.add))
        return vv

    def emit_projections(self, src):
        self.wait("tensor", "sV", self.c["sV"])
        for pk, base in (("wq_p", 0), ("wk_p", 2048)):
            for m in range(4):
                slot, idx = self.wload(self.wd[pk][m], 128, 1024)
                b = m % 2
                self.bank_acquire(b)
                self.wwait(idx)
                for kc in range(NFC):
                    mm = self.t(lambda E, kc=kc, slot=slot, b=b: E.matmul(
                        self.pb[b], slot[:, kc * 128:(kc + 1) * 128],
                        self.chunk(src, kc),
                        start=(kc == 0), stop=(kc == NFC - 1)))
                self.wretire()
                self.wait("vector", "sT", mm)
                self.v(lambda E, m=m, b=b, base=base: E.tensor_copy(
                    out=self.YQ[:, base + m * 512:base + (m + 1) * 512],
                    in_=self.pb[b]))
                self.bank_done_evac(b, "sV")
        for m in range(NFC):
            slot, idx = self.wload(self.wd["wog_p"][m], 128, 1024)
            b = m % 2
            self.bank_acquire(b)
            self.wwait(idx)
            for kc in range(NFC):
                mm = self.t(lambda E, kc=kc, slot=slot, b=b: E.matmul(
                    self.pb[b], slot[:, kc * 128:(kc + 1) * 128],
                    self.chunk(src, kc),
                    start=(kc == 0), stop=(kc == NFC - 1)))
            self.wretire()
            self.wait("scalar", "sT", mm)
            self.s(lambda E, m=m, b=b: E.activation(
                out=self.X0[:, m * 512:(m + 1) * 512], in_=self.pb[b],
                func=AF.Silu))
            self.bank_done_evac(b, "sS")
        self.bank_acquire(2)
        for kc in range(NFC):
            mm = self.t(lambda E, kc=kc: E.matmul(
                self.pb[2][0:16, :],
                self.WSMALL[:, 64 + kc * 16:64 + (kc + 1) * 16],
                self.chunk(src, kc), start=(kc == 0), stop=(kc == NFC - 1)))
        self.wait("vector", "sT", mm)
        self.v(lambda E: E.tensor_copy(out=self.U16, in_=self.pb[2][0:16, :]))
        self.bank_done_evac(2, "sV")
        uvr = self.c["sV"]
        for m in range(4):
            b = 3 + (m % 2)
            self.bank_acquire(b)
            self.wait("tensor", "sV", uvr)
            mm = self.t(lambda E, m=m, b=b: E.matmul(
                self.pb[b], self.WSMALL[0:16, 192 + m * 128:192 + (m + 1) * 128],
                self.U16, start=True, stop=True))
            self.wait("scalar", "sT", mm)
            self.s(lambda E, m=m, b=b: E.activation(
                out=self.PR[:, m * 512:(m + 1) * 512],
                in_=self.pb[b], func=AF.Sigmoid,
                bias=self.ccol(C_BGK2 + m)))
            sl = self.s(lambda E, m=m: E.activation(
                out=self.PR[:, m * 512:(m + 1) * 512],
                in_=self.PR[:, m * 512:(m + 1) * 512],
                func=AF.Ln))
            self.bank_done_evac(b, "sS")
        self.gl_ready = sl
        for kc in range(NFC):
            slot, idx = self.wload(
                self.wd["wv_nat"][kc * 128:(kc + 1) * 128, :], 128, 1024)
            self.wwait(idx)
            for pr in range(4):
                for half in range(2):
                    b = pr * 2 + half
                    if kc == 0:
                        self.bank_acquire(b)
                    mm = self.t(lambda E, kc=kc, pr=pr, half=half, b=b,
                                slot=slot: E.matmul(
                        self.pb[b],
                        src[:, kc * 512 + pr * 128:kc * 512 + (pr + 1) * 128],
                        slot[:, half * 512:(half + 1) * 512],
                        start=(kc == 0), stop=(kc == NFC - 1)))
            self.wretire()
        self.wait("vector", "sT", mm)
        for pr in range(4):
            for half in range(2):
                b = pr * 2 + half
                self.v(lambda E, pr=pr, half=half, b=b: E.tensor_copy(
                    out=self.VV[:, pr * 1024 + half * 512:
                                pr * 1024 + (half + 1) * 512],
                    in_=self.pb[b]))
                self.bank_done_evac(b, "sV")

    def emit_gla(self):
        GS = self.GS
        BP = GS[:, 0:128]; EB = GS[:, 128:256]; EM = GS[:, 256:384]
        KDX = GS[:, 384:512]
        QEXp = [GS[:, 512:640], GS[:, 640:768]]
        KADJp = [GS[:, 768:896], GS[:, 896:1024]]
        KTs = GS[:, 1024:1152]
        GAMp = [GS[:, 1152:1156], GS[:, 1156:1160]]
        ONF = GS[:, 1160:1288]
        rows = self.ROWS
        sd_row = rows[0:1, 0:128]; r_row = rows[0:1, 512:640]
        if not hasattr(self, "s_last"):
            self.s_last = {}
        units = [(pr, hd) for pr in range(4) for hd in range(4)]

        def prep(n, mmat_gate):
            pr, hd = units[n]
            par = n % 2
            QEX = QEXp[par]; KADJ = KADJp[par]
            gA = GAMp[par][:, 0:1]; gB = GAMp[par][:, 1:2]
            gAB = GAMp[par][:, 2:3]; giA = GAMp[par][:, 3:4]
            glsl = self.PR[:, hd * 512 + pr * 128:hd * 512 + (pr + 1) * 128]
            qsl = self.YQ[:, hd * 512 + pr * 128:hd * 512 + (pr + 1) * 128]
            ksl = self.YQ[:, 2048 + hd * 512 + pr * 128:
                          2048 + hd * 512 + (pr + 1) * 128]
            self.wait("vector", "sS", self.gl_ready)
            if mmat_gate:
                self.wait("vector", "sT", mmat_gate)
            self.v(lambda E: E.tensor_tensor_scan(
                out=BP[:, 0:64], data0=glsl[:, 0:64], data1=glsl[:, 0:64],
                initial=0.0, op0=ALU.add, op1=ALU.bypass))
            vbp = self.v(lambda E: E.tensor_tensor_scan(
                out=BP[:, 64:128], data0=glsl[:, 64:128], data1=glsl[:, 64:128],
                initial=0.0, op0=ALU.add, op1=ALU.bypass))
            self.wait("scalar", "sV", vbp)
            self.s(lambda E: E.activation(out=EB, in_=BP, func=AF.Exp,
                                          scale=1.0 / 16))
            self.s(lambda E: E.activation(out=EM, in_=BP, func=AF.Exp,
                                          scale=-1.0 / 16))
            self.s(lambda E: E.activation(out=gA, in_=BP[:, 63:64],
                                          func=AF.Exp, scale=1.0 / 16))
            self.s(lambda E: E.activation(out=gB, in_=BP[:, 127:128],
                                          func=AF.Exp, scale=1.0 / 16))
            sgam = self.s(lambda E: E.activation(out=giA, in_=BP[:, 63:64],
                                                 func=AF.Exp, scale=-1.0 / 16))
            self.wait("vector", "sS", sgam)
            self.v(lambda E: E.tensor_tensor(out=gAB, in0=gA, in1=gB,
                                             op=ALU.mult))
            self.v(lambda E: E.scalar_tensor_tensor(
                out=QEX[:, 0:64], in0=qsl[:, 0:64], scalar=SCALE_QK,
                in1=EB[:, 0:64], op0=ALU.mult, op1=ALU.mult))
            self.v(lambda E: E.scalar_tensor_tensor(
                out=QEX[:, 64:128], in0=qsl[:, 64:128], scalar=SCALE_QK,
                in1=EB[:, 64:128], op0=ALU.mult, op1=ALU.mult))
            self.v(lambda E: E.tensor_scalar(
                out=QEX[:, 64:128], in0=QEX[:, 64:128], scalar1=gA,
                scalar2=None, op0=ALU.mult))
            self.v(lambda E: E.tensor_tensor(
                out=KDX[:, 0:64], in0=ksl[:, 0:64], in1=EM[:, 0:64],
                op=ALU.mult))
            self.v(lambda E: E.tensor_tensor(
                out=KDX[:, 64:128], in0=ksl[:, 64:128], in1=EM[:, 64:128],
                op=ALU.mult))
            self.v(lambda E: E.tensor_scalar(
                out=KDX[:, 64:128], in0=KDX[:, 64:128], scalar1=giA,
                scalar2=None, op0=ALU.mult))
            vprep = self.v(lambda E: E.tensor_scalar(
                out=KADJ, in0=KDX, scalar1=gAB, scalar2=None, op0=ALU.mult))
            return vprep

        vprep_n = prep(0, 0)
        mmat_n = 0
        for n in range(16):
            pr, hd = units[n]
            par = n % 2
            QEX = QEXp[par]; KADJ = KADJp[par]
            gAB = GAMp[par][:, 2:3]
            ssl = self.S[:, ((pr // 2) * 4 + hd) * 256:
                         ((pr // 2) * 4 + hd + 1) * 256]
            vbase = pr * 1024 + hd * 256
            self.bank_acquire(0)
            self.wait("tensor", "sV", vprep_n)
            mmat = self.t(lambda E, KDX=KDX, QEX=QEX: E.matmul(
                self.pb[0][:, 0:128], KDX, QEX, start=True, stop=True))
            if n + 1 < 16:
                vprep_n = prep(n + 1, mmat)
            self.wait("vector", "sT", mmat)
            vamp = self.v(lambda E: E.tensor_tensor(
                out=self.AMP, in0=self.pb[0][:, 0:128],
                in1=self.ONC[:, C_MASKP:C_MASKP + 128], op=ALU.mult))
            self.bank_done_evac(0, "sV")
            self.bank_acquire(1)
            self.wait("tensor", "sV", vamp)
            self.wait("tensor", "sV", self.s_last.get(((pr // 2), hd), 0))
            for ec in range(2):
                self.t(lambda E, ec=ec, vbase=vbase: E.matmul(
                    self.pb[1][:, ec * 128:(ec + 1) * 128],
                    self.VV[:, vbase + ec * 128:vbase + (ec + 1) * 128],
                    self.AMP, start=(ec == 0), stop=False))
            for ec in range(2):
                mmo = self.t(lambda E, ec=ec, ssl=ssl, QEX=QEX: E.matmul(
                    self.pb[1][:, ec * 128:(ec + 1) * 128],
                    ssl[:, ec * 128:(ec + 1) * 128],
                    QEX, start=False, stop=(ec == 1)))
            self.bank_acquire(2)
            mmt = self.t(lambda E, KADJ=KADJ: E.transpose(
                self.pb[2][:, 0:128], KADJ,
                self.ONC[:, C_IDN:C_IDN + 128]))
            self.wait("vector", "sT", mmt)
            vkt = self.v(lambda E: E.tensor_copy(out=KTs,
                                                 in_=self.pb[2][:, 0:128]))
            self.bank_done_evac(2, "sV")
            self.bank_acquire(3)
            self.wait("tensor", "sV", vkt)
            mmds = self.t(lambda E, vbase=vbase: E.matmul(
                self.pb[3][:, 0:256], KTs,
                self.VV[:, vbase:vbase + 256], start=True, stop=True))
            self.wait("vector", "sT", mmds)
            vS = self.v(lambda E, ssl=ssl, gAB=gAB: E.scalar_tensor_tensor(
                out=ssl, in0=ssl, scalar=gAB, in1=self.pb[3][:, 0:256],
                op0=ALU.mult, op1=ALU.add))
            self.bank_done_evac(3, "sV")
            self.s_last[((pr // 2), hd)] = vS
            self.wait("scalar", "sT", mmo)
            for ec in range(2):
                ssq = self.s(lambda E, ec=ec: E.activation(
                    out=self.SQ[:, ec * 128:(ec + 1) * 128],
                    in_=self.pb[1][:, ec * 128:(ec + 1) * 128],
                    func=AF.Square))
            self.bank_acquire(4)
            self.wait("tensor", "sS", ssq)
            for ec in range(2):
                mmsq = self.t(lambda E, ec=ec: E.matmul(
                    self.pb[4][0:1, 0:128],
                    self.ONC[:, C_ONECOL:C_ONECOL + 1],
                    self.SQ[:, ec * 128:(ec + 1) * 128],
                    start=(ec == 0), stop=(ec == 1)))
            self.wait("scalar", "sT", mmsq)
            ssd = self.s(lambda E: E.activation(
                out=sd_row, in_=self.pb[4][0:1, 0:128], func=AF.Sqrt,
                scale=1.0 / 256,
                bias=self.ONC[0:1, C_EPS_RMS:C_EPS_RMS + 1]))
            self.bank_done_evac(4, "sS")
            self.wait("vector", "sS", ssd)
            self.vdrain()
            vr = self.v(lambda E: E.reciprocal(out=r_row, in_=sd_row))
            self.vdrain()
            self.bank_acquire(5)
            self.wait("tensor", "sV", vr)
            mmbc = self.t(lambda E: E.matmul(
                self.pb[5][:, 0:128],
                self.ONC[0:1, C_ONEROW:C_ONEROW + 128],
                r_row, start=True, stop=True))
            self.wait("vector", "sT", mmbc)
            self.v(lambda E: E.tensor_copy(out=self.RBC,
                                           in_=self.pb[5][:, 0:128]))
            self.bank_done_evac(5, "sV")
            for ec in range(2):
                self.v(lambda E, ec=ec: E.scalar_tensor_tensor(
                    out=ONF, in0=self.pb[1][:, ec * 128:(ec + 1) * 128],
                    scalar=self.ccol(C_GNW + ec), in1=self.RBC,
                    op0=ALU.mult, op1=ALU.mult))
                gsl = self.X0[:, (hd * 2 + ec) * 512 + pr * 128:
                              (hd * 2 + ec) * 512 + (pr + 1) * 128]
                self.v(lambda E, gsl=gsl: E.tensor_tensor(
                    out=gsl, in0=ONF, in1=gsl, op=ALU.mult))
            self.bank_done_evac(1, "sV")
            mmat_n = mmat

    def emit_wo(self):
        self.wait("tensor", "sV", self.c["sV"])
        for f in range(NFC):
            slot, idx = self.wload(self.wd["wo_p"][f], 128, 1024)
            b = f % 2
            self.bank_acquire(b)
            self.wwait(idx)
            for kc in range(NFC):
                mm = self.t(lambda E, kc=kc, slot=slot, b=b: E.matmul(
                    self.pb[b], slot[:, kc * 128:(kc + 1) * 128],
                    self.chunk(self.X0, kc),
                    start=(kc == 0), stop=(kc == NFC - 1)))
            self.wretire()
            self.wait("vector", "sT", mm)
            self.v(lambda E, f=f, b=b: E.tensor_tensor(
                out=self.chunk(self.VV, f), in0=self.pb[b],
                in1=self.chunk(self.BB, f), op=ALU.add))
            self.bank_done_evac(b, "sV")

    def mlp_pt(self, m):
        if m < 8:
            return self.X0[:, m * 512:(m + 1) * 512]
        return self.PR[:, (m - 8) * 512:(m - 7) * 512]

    def emit_mlp(self):
        self.wait("tensor", "sV", self.c["sV"])
        for hf in range(2):
            for m in range(11):
                mi = hf * 11 + m
                sg, ig = self.wload(self.wd["wgate_p"][mi], 128, 1024)
                su, iu = self.wload(self.wd["wup_p"][mi], 128, 1024)
                bA, bB = (0, 1) if (m % 2 == 0) else (2, 3)
                self.bank_acquire(bA)
                self.wwait(ig)
                for kc in range(NFC):
                    mmA = self.t(lambda E, kc=kc, sg=sg, bA=bA: E.matmul(
                        self.pb[bA], sg[:, kc * 128:(kc + 1) * 128],
                        self.chunk(self.BB, kc),
                        start=(kc == 0), stop=(kc == NFC - 1)))
                self.wretire()
                self.bank_acquire(bB)
                self.wwait(iu)
                for kc in range(NFC):
                    mmB = self.t(lambda E, kc=kc, su=su, bB=bB: E.matmul(
                        self.pb[bB], su[:, kc * 128:(kc + 1) * 128],
                        self.chunk(self.BB, kc),
                        start=(kc == 0), stop=(kc == NFC - 1)))
                self.wretire()
                self.wait("scalar", "sT", mmA)
                ss = self.s(lambda E, bA=bA: E.activation(
                    out=self.SILU, in_=self.pb[bA], func=AF.Silu))
                self.bank_done_evac(bA, "sS")
                self.wait("vector", "sS", ss)
                self.wait("vector", "sT", mmB)
                vp = self.v(lambda E, m=m, bB=bB: E.tensor_tensor(
                    out=self.mlp_pt(m),
                    in0=self.pb[bB], in1=self.SILU, op=ALU.mult))
                self.bank_done_evac(bB, "sV")
            for f in range(NFC):
                p1, i1 = self.wload(
                    self.wd["wdown_p"][f][:, hf * 1408:hf * 1408 + 1024],
                    128, 1024)
                p2, i2 = self.wload(
                    self.wd["wdown_p"][f][:, hf * 1408 + 1024:hf * 1408 + 1408],
                    128, 384)
                bY = 6 if (f % 2 == 0) else 7
                self.bank_acquire(bY)
                self.wwait(i1)
                self.wait("tensor", "sV", vp)
                for kc in range(8):
                    self.t(lambda E, kc=kc, p1=p1, bY=bY: E.matmul(
                        self.pb[bY], p1[:, kc * 128:(kc + 1) * 128],
                        self.mlp_pt(kc),
                        start=(kc == 0), stop=False))
                self.wretire()
                self.wwait(i2)
                for kc in range(3):
                    mmY = self.t(lambda E, kc=kc, p2=p2, bY=bY: E.matmul(
                        self.pb[bY], p2[:, kc * 128:(kc + 1) * 128],
                        self.mlp_pt(8 + kc),
                        start=False, stop=(kc == 2)))
                self.wretire()
                self.wait("vector", "sT", mmY)
                if hf == 0:
                    self.v(lambda E, f=f, bY=bY: E.tensor_copy(
                        out=self.chunk(self.HX, f), in_=self.pb[bY]))
                else:
                    self.v(lambda E, f=f, bY=bY: E.tensor_tensor(
                        out=self.chunk(self.HX, f), in0=self.pb[bY],
                        in1=self.chunk(self.HX, f), op=ALU.add))
                    self.v(lambda E, f=f: E.tensor_tensor(
                        out=self.chunk(self.HX, f), in0=self.chunk(self.HX, f),
                        in1=self.chunk(self.VV, f), op=ALU.add))
                self.bank_done_evac(bY, "sV")

    def emit_add(self, a, b, dst):
        for c in range(NFC):
            vv = self.v(lambda E, c=c: E.tensor_tensor(
                out=self.chunk(dst, c), in0=self.chunk(a, c),
                in1=self.chunk(b, c), op=ALU.add))
        return vv

    def emit_tile(self, i, store_prev):
        p = self
        p.wait("vector", "sGS", store_prev)
        if i == 0:
            p.load_xtile(i, waits={"sV": p.c["sV"], "sT": p.c["sT"]})
            self.xg_pending = p.c["sGX"]
        xg = self.xg_pending
        for e in ("tensor", "scalar", "vector"):
            p.wait(e, "sGX", xg)
        p.emit_norm(p.X0, p.BB, "ln", eps_col=C_EPS_LN)
        if SPARSE_MOE:
            p.emit_moe_pre(p.BB, p.X1)
        p.emit_routing(p.BB)
        if SPARSE_MOE:
            p.emit_moe_sparse(p.BB, p.YQ, p.X1, p.VV)
        else:
            p.emit_moe(p.BB, p.YQ)
        p.emit_resid_half(p.X0, p.YQ, p.X1)
        p.emit_norm(p.X1, p.BB, "ln", eps_col=C_EPS_LN,
                    src_ready={"sV": p.c["sV"]})
        p.emit_norm(p.BB, p.HX, "rms", w_col0=C_ANW, eps_col=C_EPS_RMS,
                    src_ready={"sV": p.c["sV"]})
        p.emit_projections(p.HX)
        p.emit_gla()
        p.emit_wo()
        p.emit_norm(p.VV, p.BB, "rms", w_col0=C_MNW, eps_col=C_EPS_RMS,
                    src_ready={"sV": p.c["sV"]})
        p.emit_mlp()
        if i + 1 < self.ntiles:
            p.load_xtile(i + 1, waits={"sV": p.c["sV"], "sT": p.c["sT"]})
        self.xg_pending = p.c["sGX"]
        p.emit_add(p.HX, p.X1, p.HX)
        p.emit_norm(p.HX, p.BB, "ln", eps_col=C_EPS_LN,
                    src_ready={"sV": p.c["sV"]})
        if SPARSE_MOE:
            p.emit_moe_pre(p.BB, p.X1)
        p.emit_routing(p.BB)
        if SPARSE_MOE:
            p.emit_moe_sparse(p.BB, p.YQ, p.X1, p.VV)
        else:
            p.emit_moe(p.BB, p.YQ)
        p.emit_resid_half(p.HX, p.YQ, p.X1)
        return p.store_tile(i, p.X1)

    def tap(self, name, ap_src, shape):
        if name not in self.debug_taps:
            return
        ap = self.nc.dram_tensor("dbg_" + name, list(shape), F32,
                                 kind="ExternalOutput").ap()
        self.dbg[name] = ap
        self.gdma(ap, ap_src, sem="sGS",
                  waits={"sV": self.c["sV"], "sT": self.c["sT"],
                         "sS": self.c["sS"]})
        for e in ("vector", "scalar", "tensor"):
            self.wait(e, "sGS", self.c["sGS"])

    def finalize(self, store_total):
        nc = self.nc
        for ring in ([self.wring] + ([self.bring] if BF16_EXPERTS else [])):
            assert len(ring.retire) == ring.n, (ring.name, len(ring.retire), ring.n)
            assert sorted(ring.retire) == list(range(ring.n)), ring.name
        def emit_fill(E, ring, src, part, width, idx, gate):
            if idx >= ring.R:
                sname, cnt = ring.retire[idx - ring.R]
                if cnt > 0:
                    E.wait_ge(self.sem[sname], cnt)
            if gate is not None:
                for gn, gv in gate:
                    E.wait_ge(self.sem[gn], gv)
            E.dma_start(out=ring.slots[idx % ring.R][0:part, 0:width],
                        in_=src).then_inc(ring.sems[idx % ring.R], 16)

        def sync_body(E):
            for (ring, src, part, width, idx, gate) in self.fill_seq:
                if not (BF16_EXPERTS and ring is self.bring):
                    emit_fill(E, ring, src, part, width, idx, gate)
        with nc.Block() as block:
            @block.tensor
            def _(E):
                for fn in self.ops["tensor"]:
                    fn(E)
            @block.vector
            def _(E):
                for fn in self.ops["vector"]:
                    fn(E)
            @block.scalar
            def _(E):
                for fn in self.ops["scalar"]:
                    fn(E)
            @block.sync
            def _(E):
                sync_body(E)
            @block.gpsimd
            def _(E):
                for fn in self.ops["gpsimd"]:
                    fn(E)
                if store_total:
                    E.wait_ge(self.sem["sGS"], store_total)
        return nc

_CACHE = {}


def _build_full():
    if "nc" not in _CACHE:
        p = Prog(ntiles=NT)
        p.emit_preamble()
        if SPARSE_MOE and BF16_EXPERTS:
            p.emit_bf16_staging()
        total = 0
        for i in range(NT):
            total = p.emit_tile(i, total)
        nc = p.finalize(total)
        _CACHE["nc"] = nc
    return _CACHE["nc"]


def kernel(**inputs):
    nc = _build_full()
    x = np.asarray(inputs["L0_attn_hidden_states"], np.float32)
    consts = host_consts(inputs["L0_attn_attn_norm_w"],
                         inputs["L0_attn_mlp_norm_w"],
                         inputs["L0_attn_gnorm_w"],
                         inputs["L0_attn_bgk2"])
    w = pack_all_weights(inputs)
    in_maps = []
    for c in range(8):
        xc = np.ascontiguousarray(x[2 * c:2 * c + 2].transpose(0, 2, 1))
        in_maps.append({"x": xc, "consts": consts, **w})
    res = run_bass_kernel_spmd(nc, in_maps, core_ids=list(range(8)))
    outs = [np.asarray(res.results[c]["out"]).transpose(0, 2, 1)
            for c in range(8)]
    return np.ascontiguousarray(np.concatenate(outs, 0))
